# version 24
# baseline (speedup 1.0000x reference)
import numpy as np
import concourse.bass as bass
import concourse.mybir as mybir
from concourse.bass_utils import run_bass_kernel_spmd
from concourse.alu_op_type import AluOpType as OP
from contextlib import ExitStack

F32 = mybir.dt.float32
BF16 = mybir.dt.bfloat16
AF = mybir.ActivationFunctionType
AX = mybir.AxisListType

S, D, H, HD, W = 2048, 1024, 16, 64, 512
C = S // W
NT = S // 128
KC = D // 128
E, IH = 8, 2048
HC = IH // 128
NP = 8
EPS, ALPHA, LIMIT, NEG = 1e-5, 1.702, 7.0, -1.0e30
CSAT = float(ALPHA * LIMIT / (1.0 + np.exp(-ALPHA * LIMIT)))
ENGINES = ("sync", "gpsimd", "vector", "scalar", "tensor")


_DT_SIZE = {F32: 4, BF16: 2}
_FENCED = ("vector", "scalar", "gpsimd")


def _footprints(inst):
    nm = type(inst).__name__
    if "DMA" in nm or "Dma" in nm:
        return [], []

    def rng(arg):
        if not hasattr(arg, "ap") or not hasattr(arg, "memref"):
            return None
        sz = _DT_SIZE.get(arg.dtype, 4)
        pairs = list(arg.ap)
        hi = arg.offset
        for (st, cnt) in pairs[1:]:
            hi += abs(int(st)) * (int(cnt) - 1)
        return (arg.memref, int(arg.offset) * sz, (hi + 1) * sz)

    rds = [r for r in (rng(a) for a in inst.ins) if r is not None]
    wrs = [r for r in (rng(a) for a in inst.outs) if r is not None]
    return rds, wrs


def _overlap(aa, bb):
    for (n1, l1, h1) in aa:
        for (n2, l2, h2) in bb:
            if n1 == n2 and l1 < h2 and l2 < h1:
                return True
    return False


def _fence_plan(seq):
    fences = set()
    pw, pr = [], []
    for i, (rd, wr) in enumerate(seq):
        if _overlap(rd, pw) or _overlap(wr, pw) or _overlap(wr, pr):
            fences.add(i)
            pw, pr = [], []
        pw += wr
        pr += rd
    return fences


class Ctx:

    def __init__(self, name, sems, collect=None, fences=None):
        self.a = name
        self.e = None
        self.cnt = {k: 0 for k in sems}
        self.sems = sems
        self.R = {}
        self.collect = collect
        self.fences = fences if fences is not None else set()
        self.idx = 0

    def op(self, eng, fn, inc=None):
        incs = [] if inc is None else (inc if isinstance(inc, list) else [inc])
        instr = None
        if eng == self.a:
            if eng in _FENCED:
                if self.idx in self.fences:
                    self.e.drain()
                self.idx += 1
            instr = fn(self.e)
            if self.collect is not None and eng in _FENCED:
                self.collect.append(_footprints(instr.ins))
        for (sn, k) in incs:
            self.cnt[sn] += k
            if instr is not None:
                instr.then_inc(self.sems[sn], k)

    def wait(self, eng, sn, val):
        if val and val > 0 and eng == self.a:
            self.e.wait_ge(self.sems[sn], val)

    def n(self, sn):
        return self.cnt[sn]


def build_nc(dump=None, race_debug=False, _fences=None, _collect=None):
    nc = bass.Bass("TRN2", target_bir_lowering=False, debug=False,
                   detect_race_conditions=bool(race_debug))

    x_d = nc.dram_tensor("x", [S, D], F32, kind="ExternalInput")
    wqkv_d = nc.dram_tensor("wqkv", [D, 3 * D], F32, kind="ExternalInput")
    wout_d = nc.dram_tensor("wout", [D, D], F32, kind="ExternalInput")
    ns_d = nc.dram_tensor("ns", [128, KC], F32, kind="ExternalInput")
    gw_d = nc.dram_tensor("gatew", [128, KC * E], F32, kind="ExternalInput")
    gb_d = nc.dram_tensor("gateb", [128, E], F32, kind="ExternalInput")
    wglu_d = nc.dram_tensor("wglu", [E, NP, 128, KC * 256], F32, kind="ExternalInput")
    wlin_d = nc.dram_tensor("wlin", [E, NP, 128, KC * 256], F32, kind="ExternalInput")
    b1g_d = nc.dram_tensor("b1g", [E, 128, HC], F32, kind="ExternalInput")
    b1l_d = nc.dram_tensor("b1l", [E, 128, HC], F32, kind="ExternalInput")
    w2_d = nc.dram_tensor("w2", [E, IH, D], F32, kind="ExternalInput")
    b2_d = nc.dram_tensor("b2", [E, D], F32, kind="ExternalInput")
    y_d = nc.dram_tensor("y", [S, D], F32, kind="ExternalOutput")
    dump_d = None
    if dump is not None:
        dump_d = nc.dram_tensor("dbg", list(dump[1]), dump[2], kind="ExternalOutput")

    es = ExitStack()
    sb = lambda n, sh, dt: es.enter_context(nc.sbuf_tensor(n, sh, dt))

    xres = sb("xres", [128, NT * D], F32)
    nst = sb("nst", [128, KC], F32)
    gwt = sb("gwt", [128, KC * E], F32)
    gbt = sb("gbt", [128, E], F32)
    b1gt = sb("b1gt", [128, E * HC], F32)
    b1lt = sb("b1lt", [128, E * HC], F32)
    b2t = sb("b2t", [E, D], BF16)
    idf = sb("idf", [128, 128], F32)
    stat = sb("stat", [128, 16], F32)
    vecs = sb("vecs", [128, 96], F32)
    vb16 = sb("vb16", [128, 16], BF16)
    gl = sb("gl", [128, 64], F32)
    gm = sb("gm", [128, 64], F32)

    sems = {k: es.enter_context(nc.semaphore(k))
            for k in ("s_ld", "s_x", "s_gw", "s_pe", "s_ve", "s_ac", "s_gp", "s_out",
                      "s_qk0", "s_qk1", "s_w1a", "s_w1b", "s_w2")}
    ctxs = {name: Ctx(name, sems,
                      collect=(_collect.setdefault(name, []) if _collect is not None else None),
                      fences=(_fences or {}).get(name))
            for name in ENGINES}

    def dump_tile(c, key, src_ap, row0, nrow, wsem=None, wval=0):
        if dump is None or dump[0] != key:
            return
        if wsem is not None:
            c.wait("scalar", wsem, wval)
        c.op("scalar", lambda e: e.dma_start(
            out=dump_d[row0:row0 + nrow, :], in_=src_ap), inc=("s_out", 16))
        for eng in ("vector", "scalar", "tensor", "gpsimd"):
            c.wait(eng, "s_out", c.n("s_out"))

    es0 = ExitStack()
    scr = es0.enter_context(nc.sbuf_tensor("scr0", [128, 64], F32))
    scrb = es0.enter_context(nc.sbuf_tensor("scr0b", [128, 64], BF16))

    esP = ExitStack()
    pp = lambda n, sh, dt: esP.enter_context(nc.psum_tensor(n, sh, dt))
    psS = [None, None]
    psS[0] = pp("psSa", [128, 1024], F32)
    psS[1] = pp("psSb", [128, 1024], F32)
    psT = [pp("psT0", [128, 512], F32), pp("psT1", [128, 512], F32)]
    psO = pp("psO", [128, 512], F32)
    psM = pp("psM", [128, 512], F32)
    psMb = psM[:, :].bitcast(BF16)
    psOb = psO[:, :].bitcast(BF16)

    ps0 = psS[0][:, 0:512]

    esA = ExitStack()
    sA = lambda n, sh, dt: esA.enter_context(nc.sbuf_tensor(n, sh, dt))
    idb = sA("idb", [128, 128], BF16)
    mq = sA("mq", [128, 128], F32)
    mk = sA("mk", [128, 128], F32)
    onesr = sA("onesr", [1, 128], F32)
    irow = sA("irow", [128, 128], F32)
    icol = sA("icol", [128, 1], F32)
    wvb = sA("wvb", [128, KC * D], BF16)
    wqk = [sA(f"wqk{i}", [128, KC * 256], BF16) for i in range(2)]
    woutb = sA("woutb", [128, KC * D], BF16)
    t1tm = sA("t1tm", [128, 2 * D], BF16)
    t1fm = sA("t1fm", [128, KC * W], BF16)
    qsep = sA("qsep", [128, 16 * W], BF16)
    kaug = [sA(f"kaug{i}", [128, 16 * W], BF16) for i in range(2)]
    vbuf = [sA(f"vbuf{i}", [128, 4 * D], BF16) for i in range(2)]
    qexp = sA("qexp", [128, 1024], BF16)
    attnT = sA("attnT", [128, 3 * W], BF16)
    aout = sA("aout", [128, 8 * W], BF16)
    tmpa = sA("tmpa", [128, D], F32)

    def walk0(c):
        R = c.R
        c.op("sync", lambda e: e.dma_start(
            out=xres[:, 0:4 * D].rearrange("p (t d) -> p t d", d=D),
            in_=x_d[0:W, :].rearrange("(t p) d -> p t d", p=128)), inc=("s_x", 16))
        for cc in range(1, C):
            c.op("sync", lambda e, cc=cc: e.dma_start(
                out=xres[:, cc * 4 * D:(cc + 1) * 4 * D].rearrange("p (t d) -> p t d", d=D),
                in_=x_d[cc * W:(cc + 1) * W, :].rearrange("(t p) d -> p t d", p=128)),
                inc=("s_ld", 16))
        for (dst, src) in ((nst, ns_d), (gwt, gw_d), (gbt, gb_d)):
            c.op("sync", lambda e, dst=dst, src=src: e.dma_start(out=dst[:, :], in_=src[:, :]),
                 inc=("s_ld", 16))
        c.op("sync", lambda e: e.dma_start(
            out=b1gt[:, :].rearrange("p (e h) -> p e h", e=E),
            in_=b1g_d[:, :, :].transpose((1, 0, 2))), inc=("s_ld", 16))
        c.op("sync", lambda e: e.dma_start(
            out=b1lt[:, :].rearrange("p (e h) -> p e h", e=E),
            in_=b1l_d[:, :, :].transpose((1, 0, 2))), inc=("s_ld", 16))
        R["ld0"] = c.n("s_ld")

        c.op("gpsimd", lambda e: e.iota(icol[:, :], [[0, 1]], channel_multiplier=1,
                                        allow_small_or_imprecise_dtypes=True), inc=("s_gp", 1))
        c.op("gpsimd", lambda e: e.iota(irow[:, :], [[1, 128]], channel_multiplier=0,
                                        allow_small_or_imprecise_dtypes=True), inc=("s_gp", 1))
        for kc in range(KC):
            c.op("gpsimd", lambda e, kc=kc: e.dma_start(
                out=wvb[:, kc * D:(kc + 1) * D],
                in_=wqkv_d[kc * 128:(kc + 1) * 128, 2 * D:3 * D]), inc=("s_gw", 16))
        for kc in range(KC):
            c.op("gpsimd", lambda e, kc=kc: e.dma_start(
                out=woutb[:, kc * D:(kc + 1) * D],
                in_=wout_d[kc * 128:(kc + 1) * 128, :]), inc=("s_gw", 16))
        c.op("gpsimd", lambda e: e.dma_start(out=b2t[:, :], in_=b2_d[:, :]), inc=("s_gw", 16))
        c.op("gpsimd", lambda e: e.dma_start(
            out=wqk[0][:, :].rearrange("p (k j) -> p k j", k=KC)[:, :, 0:128],
            in_=wqkv_d[:, :].rearrange("(k p) m -> p k m", p=128)[:, :, 0:128]), inc=("s_qk0", 16))
        c.op("gpsimd", lambda e: e.dma_start(
            out=wqk[0][:, :].rearrange("p (k j) -> p k j", k=KC)[:, :, 128:256],
            in_=wqkv_d[:, :].rearrange("(k p) m -> p k m", p=128)[:, :, D:D + 128]), inc=("s_qk0", 16))
        R["gw0"] = c.n("s_gw")
        R[("qkb", 0)] = ("s_qk0", c.n("s_qk0"))
        R[("qkb", 1)] = ("s_qk1", 0)

        c.wait("vector", "s_gp", 2)
        c.wait("vector", "s_ld", R["ld0"])
        c.op("vector", lambda e: e.tensor_scalar(mq[:, :], irow[:, :], icol[:, :], NEG, OP.is_gt, OP.mult))
        c.op("vector", lambda e: e.tensor_scalar(mk[:, :], irow[:, :], icol[:, :], NEG, OP.is_lt, OP.mult))
        c.op("vector", lambda e: e.tensor_scalar(idf[:, :], irow[:, :], icol[:, :], None, OP.is_equal))
        c.op("vector", lambda e: e.tensor_copy(idb[:, :], idf[:, :]))
        c.op("vector", lambda e: e.memset(onesr[:, :], 1.0))
        c.op("vector", lambda e: e.memset(qsep[64:128, :], 0.0))
        for kk in range(2):
            c.op("vector", lambda e, kk=kk: e.memset(kaug[kk][64:128, :], 0.0))
            c.op("vector", lambda e, kk=kk: e.memset(kaug[kk][64:66, :], 1.0))
        c.op("vector", lambda e: e.tensor_scalar(b1gt[:, :], b1gt[:, :], ALPHA, None, OP.mult))
        c.op("vector", lambda e: e.bn_stats(scr[:, 0:6], irow[:, 0:32]))
        c.op("vector", lambda e: e.bn_aggr(scr[:, 6:8], scr[:, 0:6]))
        c.op("vector", lambda e: e.tensor_reduce(scr[:, 8:9], irow[:, 0:32], AX.X, OP.max, negate=True))
        c.op("vector", lambda e: e.scalar_tensor_tensor(scr[:, 9:10], icol[:, :], 1.0, icol[:, :], OP.mult, OP.add))
        c.op("vector", lambda e: e.reciprocal(scr[:, 10:11], scr[:, 9:10]))
        c.op("vector", lambda e: e.tensor_copy(scrb[:, 0:32], irow[:, 0:32]), inc=("s_ve", 1))
        R["ve0"] = c.n("s_ve")

        c.wait("tensor", "s_ve", R["ve0"])
        c.op("tensor", lambda e: e.matmul(ps0[0:16, 0:16], scrb[:, 0:16], scrb[:, 16:32], start=True, stop=True))
        c.op("tensor", lambda e: e.transpose(ps0[:, 128:256], idf[:, :], idf[:, :]))
        c.op("tensor", lambda e: e.transpose(ps0[:, 256:320].bitcast(BF16), idb[:, :], idb[:, :]),
             inc=("s_pe", 1))
        R["pe0"] = c.n("s_pe")

        c.wait("scalar", "s_ve", R["ve0"])
        c.op("scalar", lambda e: e.activation(scr[:, 16:17], scr[:, 9:10], AF.Exp, scale=-0.01))
        c.op("scalar", lambda e: e.activation(scr[:, 17:18], scr[:, 16:17], AF.Ln))
        c.op("scalar", lambda e: e.activation(scr[:, 18:19], scr[:, 9:10], AF.Silu), inc=("s_ac", 1))
        c.wait("scalar", "s_pe", R["pe0"])
        c.op("scalar", lambda e: e.copy(scr[:, 19:20], ps0[:, 128:129]))

    def ln_tile(c, src_ap, wait_src=None):
        if wait_src is not None:
            for (sn, v) in wait_src:
                c.wait("vector", sn, v)
        c.op("vector", lambda e: e.bn_stats(stat[:, 0:6], src_ap[:, 0:512]))
        c.op("vector", lambda e: e.bn_stats(stat[:, 6:12], src_ap[:, 512:1024]))
        c.op("vector", lambda e: e.bn_aggr(stat[:, 12:14], stat[:, 0:12]))
        c.op("vector", lambda e: e.tensor_scalar(stat[:, 14:15], stat[:, 13:14], EPS, None, OP.add),
             inc=("s_ve", 1))
        c.wait("scalar", "s_ve", c.n("s_ve"))
        c.op("scalar", lambda e: e.activation(stat[:, 15:16], stat[:, 14:15], AF.Ln))
        c.op("scalar", lambda e: e.activation(stat[:, 15:16], stat[:, 15:16], AF.Exp, scale=-0.5),
             inc=("s_ac", 1))
        c.wait("vector", "s_ac", c.n("s_ac"))

    def walkA(c):
        R = c.R

        def a5_block(c5):
            for j in range(4):
                tt = 4 * c5 + j
                pW = psS[j % 2]
                c.wait("tensor", "s_ve", R[("aev", c5, KC - 1)])
                if j < 2:
                    c.wait("tensor", "s_ac", R[("qexp", c5, KC - 1)])
                else:
                    c.wait("tensor", "s_ve", R[("xr", c5, j - 2)])
                for dk in range(KC):
                    for t in range(2):
                        c.op("tensor", lambda e, dk=dk, t=t, j=j, pW=pW: e.matmul(
                            pW[:, t * 512:(t + 1) * 512],
                            aout[:, dk * W + j * 128:dk * W + (j + 1) * 128],
                            woutb[:, dk * D + t * 512:dk * D + (t + 1) * 512],
                            start=(dk == 0), stop=(dk == KC - 1)),
                            inc=("s_pe", 1) if (dk == KC - 1 and t == 1) else None)
                R[("wo", c5, j)] = c.n("s_pe")
                ln_tile(c, pW, wait_src=[("s_pe", R[("wo", c5, j)])])
                c.op("vector", lambda e, pW=pW: e.tensor_scalar(
                    tmpa[:, :], pW[:, :], stat[:, 12:13], stat[:, 15:16], OP.subtract, OP.mult))
                c.op("vector", lambda e, tt=tt: e.tensor_tensor(
                    xres[:, tt * D:(tt + 1) * D], xres[:, tt * D:(tt + 1) * D], tmpa[:, :], OP.add),
                    inc=("s_ve", 1))
                R[("xr", c5, j)] = c.n("s_ve")
        c.wait("gpsimd", "s_gw", 0)
        for cc in range(C):
            cur, prv = cc % 2, 1 - cc % 2
            if cc == 0:
                c.wait("vector", "s_x", 16)
                c.wait("vector", "s_gp", 2)
            elif cc == 1:
                c.wait("vector", "s_ld", R["ld0"])
            for j in range(4):
                tt = 4 * cc + j
                c.op("vector", lambda e, tt=tt: e.bn_stats(stat[:, 0:6], xres[:, tt * D:tt * D + 512]))
                c.op("vector", lambda e, tt=tt: e.bn_stats(stat[:, 6:12], xres[:, tt * D + 512:(tt + 1) * D]))
                c.op("vector", lambda e, j=j: e.bn_aggr(vecs[:, 32 + 2 * j:34 + 2 * j], stat[:, 0:12]))
            vpr = vecs[:, 32:40].rearrange("p (a b) -> p a b", b=2)
            c.op("vector", lambda e: e.tensor_scalar(
                vecs[:, 40:44], vpr[:, :, 1], EPS, None, OP.add), inc=("s_ve", 1))
            c.wait("scalar", "s_ve", c.n("s_ve"))
            c.op("scalar", lambda e: e.activation(vecs[:, 40:44], vecs[:, 40:44], AF.Ln))
            c.op("scalar", lambda e: e.activation(vecs[:, 40:44], vecs[:, 40:44], AF.Exp, scale=-0.5),
                 inc=("s_ac", 1))
            c.wait("vector", "s_ac", c.n("s_ac"))
            for j in range(4):
                tt = 4 * cc + j
                g_ = 4 * cc + j
                if g_ >= 2:
                    c.wait("vector", "s_pe", R[("t1T", (g_ - 2) // 4, (g_ - 2) % 4)])
                c.op("vector", lambda e, tt=tt, j=j: e.tensor_scalar(
                    t1tm[:, (j % 2) * D:(j % 2 + 1) * D], xres[:, tt * D:(tt + 1) * D],
                    vecs[:, 32 + 2 * j:33 + 2 * j], vecs[:, 40 + j:41 + j], OP.subtract, OP.mult), inc=("s_ve", 1))
                R[("t1", cc, j)] = c.n("s_ve")
                c.wait("tensor", "s_ve", R[("t1", cc, j)])
                if cc > 0 and j == 0:
                    c.wait("tensor", "s_ac", R[("expT", cc - 1)])
                    c.wait("tensor", "s_ve", R[("b2r", cc - 1, KC - 1, 1)])
                elif (cc, j) != (0, 0):
                    pj = (cc * 4 + j) - 1
                    c.wait("tensor", "s_ve", R[("t1fm", pj // 4, pj % 4)])
                else:
                    c.wait("tensor", "s_pe", R["pe0"])
                for kc in range(KC):
                    c.op("tensor", lambda e, j=j, kc=kc: e.transpose(
                        psMb[:, kc * 128:(kc + 1) * 128],
                        t1tm[:, (j % 2) * D + kc * 128:(j % 2) * D + (kc + 1) * 128], idb[:, :]),
                        inc=("s_pe", 1) if kc == KC - 1 else None)
                R[("t1T", cc, j)] = c.n("s_pe")
                c.wait("vector", "s_pe", R[("t1T", cc, j)])
                c.op("vector", lambda e, j=j: e.tensor_copy(
                    t1fm[:, :].rearrange("p (k w) -> p k w", k=KC)[:, :, j * 128:(j + 1) * 128],
                    psMb[:, :].rearrange("p (k t) -> p k t", k=KC)), inc=("s_ve", 1))
                R[("t1fm", cc, j)] = c.n("s_ve")
            if cc == 0:
                dump_tile(c, "t1fm", t1fm[:, 0:KC * W], 0, 128, "s_ve", R[("t1fm", 0, 3)])

            if cc > 0:
                a5_block(cc - 1)
            c.wait("tensor", "s_ve", R[("t1fm", cc, 3)])
            c.wait("tensor", "s_gw", R["gw0"])
            if cc > 0:
                c.wait("tensor", "s_ve", R[("xr", cc - 1, 3)])
                c.wait("tensor", "s_ac", R[("expT", cc - 1)])
            for j in range(4):
                if j >= 2:
                    c.wait("tensor", "s_ve", R[("vev", cc, j - 2)])
                for kc in range(KC):
                    for t in range(2):
                        c.op("tensor", lambda e, j=j, kc=kc, t=t: e.matmul(
                            psS[j % 2][:, t * 512:(t + 1) * 512],
                            t1fm[:, kc * W + j * 128:kc * W + (j + 1) * 128],
                            wvb[:, kc * D + t * 512:kc * D + (t + 1) * 512],
                            start=(kc == 0), stop=(kc == KC - 1)),
                            inc=("s_pe", 1) if (kc == KC - 1 and t == 1) else None)
                R[("vmm", cc, j)] = c.n("s_pe")
                c.wait("vector", "s_pe", R[("vmm", cc, j)])
                c.op("vector", lambda e, j=j, cur=cur: e.tensor_copy(
                    vbuf[cur][:, j * D:(j + 1) * D], psS[j % 2][:, :]), inc=("s_ve", 1))
                R[("vev", cc, j)] = c.n("s_ve")
            for hp in range(KC):
                buf = hp % 2
                nxt = hp + 1
                if nxt < KC or cc < C - 1:
                    nhp = nxt % KC
                    nsem = "s_qk%d" % (1 - buf)
                    if hp >= 1:
                        c.wait("gpsimd", "s_pe", R[("qk_mm", cc, hp - 1)])
                    elif cc > 0:
                        c.wait("gpsimd", "s_pe", R[("qk_mm", cc - 1, KC - 1)])
                    c.op("gpsimd", lambda e, nhp=nhp: e.dma_start(
                        out=wqk[1 - buf][:, :].rearrange("p (k j) -> p k j", k=KC)[:, :, 0:128],
                        in_=wqkv_d[:, :].rearrange("(k p) m -> p k m", p=128)[:, :, nhp * 128:(nhp + 1) * 128]),
                        inc=(nsem, 16))
                    c.op("gpsimd", lambda e, nhp=nhp: e.dma_start(
                        out=wqk[1 - buf][:, :].rearrange("p (k j) -> p k j", k=KC)[:, :, 128:256],
                        in_=wqkv_d[:, :].rearrange("(k p) m -> p k m", p=128)[:, :, D + nhp * 128:D + (nhp + 1) * 128]),
                        inc=(nsem, 16))
                    R[("qkb", 1 - buf)] = (nsem, c.n(nsem))
                wsem, wval = R[("qkb", buf)]
                c.wait("tensor", wsem, wval)
                if hp >= 1:
                    c.wait("tensor", "s_ve", R[("kev", cc, hp - 1)])
                elif cc > 0:
                    c.wait("tensor", "s_ve", R[("kev", cc - 1, KC - 1)])
                pt_q, pt_k = psT[hp % 2], psT[1 - hp % 2]
                for (qk, pt) in ((0, pt_q), (1, pt_k)):
                    for kc in range(KC):
                        c.op("tensor", lambda e, qk=qk, pt=pt, kc=kc, buf=buf: e.matmul(
                            pt[:, :],
                            wqk[buf][:, kc * 256 + qk * 128:kc * 256 + (qk + 1) * 128],
                            t1fm[:, kc * W:(kc + 1) * W],
                            start=(kc == 0), stop=(kc == KC - 1)),
                            inc=("s_pe", 1) if kc == KC - 1 else None)
                R[("qk_mm", cc, hp)] = c.n("s_pe")
                c.wait("vector", "s_pe", R[("qk_mm", cc, hp)])
                for hh in range(2):
                    c.op("vector", lambda e, hp=hp, hh=hh, pt_q=pt_q: e.tensor_scalar(
                        qsep[0:64, (2 * hp + hh) * W:(2 * hp + hh + 1) * W],
                        pt_q[hh * 64:hh * 64 + 64, :], 1.0 / 8.0, None, OP.mult),
                        inc=("s_ve", 1))
                    c.op("vector", lambda e, hp=hp, hh=hh, cur=cur, pt_k=pt_k: e.tensor_copy(
                        kaug[cur][0:64, (2 * hp + hh) * W:(2 * hp + hh + 1) * W],
                        pt_k[hh * 64:hh * 64 + 64, :]), inc=("s_ve", 1))
                R[("kev", cc, hp)] = c.n("s_ve")
            if cc == 0:
                dump_tile(c, "qkv_q", qsep[0:64, 0:8 * W], 0, 64, "s_ve", R[("kev", 0, 7)])
                dump_tile(c, "qkv_k", kaug[cur][0:64, 0:8 * W], 0, 64, "s_ve", R[("kev", 0, 7)])
                dump_tile(c, "qkv_v", vbuf[cur][:, 0:4 * D], 0, 128, "s_ve", R[("kev", 0, 7)])

            klo = 0 if cc > 0 else 512
            kqs = list(range(8)) if cc > 0 else list(range(4, 8))
            nk = len(kqs)
            nseq = 2 * nk

            def qside_unit(hp, qseq):
                hh, qt = qseq // 4, qseq % 4
                hc0 = (2 * hp + hh) * W
                pS = psS[qseq % 2]
                if qseq >= 2:
                    c.wait("tensor", "s_ac", R[("qexpi", cc, hp, qseq - 2)])
                elif hp == 0:
                    c.wait("tensor", "s_ve", R[("kev", cc, 0)])
                else:
                    c.wait("tensor", "s_ac", R[("qexpi", cc, hp - 1, 6 + qseq)])
                if cc > 0:
                    c.op("tensor", lambda e, hc0=hc0, qt=qt, pS=pS, prv=prv: e.matmul(
                        pS[:, 0:512],
                        qsep[0:64, hc0 + qt * 128:hc0 + (qt + 1) * 128],
                        kaug[prv][0:64, hc0:hc0 + W],
                        start=True, stop=True))
                c.op("tensor", lambda e, hc0=hc0, qt=qt, pS=pS, cur=cur: e.matmul(
                    pS[:, 512:512 + 128 * (qt + 1)],
                    qsep[0:64, hc0 + qt * 128:hc0 + (qt + 1) * 128],
                    kaug[cur][0:64, hc0:hc0 + 128 * (qt + 1)],
                    start=True, stop=True), inc=("s_pe", 1))
                R[("qmi", cc, hp, qseq)] = c.n("s_pe")
                c.wait("vector", "s_pe", R[("qmi", cc, hp, qseq)])
                if hp > 0 and qseq == 0:
                    c.wait("vector", "s_ac", R[("qexp", cc, hp - 1)])
                c.op("vector", lambda e, pS=pS, qt=qt: e.tensor_tensor(
                    pS[:, 512 + qt * 128:512 + (qt + 1) * 128],
                    pS[:, 512 + qt * 128:512 + (qt + 1) * 128], mq[:, :], OP.add))
                c.op("vector", lambda e, pS=pS, qt=qt, hh=hh: e.tensor_reduce(
                    vecs[:, hh * 4 + qt:hh * 4 + qt + 1],
                    pS[:, klo:512 + 128 * (qt + 1)], AX.X, OP.max, negate=True),
                    inc=("s_ve", 1))
                R[("qred", cc, hp, qseq)] = c.n("s_ve")
                c.wait("scalar", "s_ve", R[("qred", cc, hp, qseq)])
                c.op("scalar", lambda e, pS=pS, qt=qt, hh=hh: e.activation(
                    qexp[:, 0:512 + 128 * (qt + 1) - klo],
                    pS[:, klo:512 + 128 * (qt + 1)], AF.Exp,
                    bias=vecs[:, hh * 4 + qt:hh * 4 + qt + 1],
                    accum_out=vecs[:, 8 + hh * 4 + qt:8 + hh * 4 + qt + 1]),
                    inc=("s_ac", 1))
                R[("qexpi", cc, hp, qseq)] = c.n("s_ac")
                if qseq == 7:
                    R[("qexp", cc, hp)] = c.n("s_ac")

            def bias2_chain(hp, hh):
                s0 = hh * 4
                if hp > 0:
                    c.wait("scalar", "s_ve", R[("b2c", cc, hp - 1, hh)])
                c.op("scalar", lambda e, s0=s0: e.activation(
                    vecs[:, 16 + s0:20 + s0], vecs[:, 8 + s0:12 + s0], AF.Ln), inc=("s_ac", 1))
                c.wait("vector", "s_ac", c.n("s_ac"))
                if hp > 0:
                    c.wait("vector", "s_pe", R[("b2T", cc, hp - 1, hh)])
                c.op("vector", lambda e, s0=s0: e.tensor_tensor(
                    vecs[:, 24 + s0:28 + s0], vecs[:, s0:4 + s0], vecs[:, 16 + s0:20 + s0], OP.subtract))
                c.op("vector", lambda e, s0=s0: e.tensor_copy(
                    vb16[:, 0:8][:, s0:s0 + 4], vecs[:, 24 + s0:28 + s0]))
                c.op("vector", lambda e, s0=s0: e.tensor_tensor(
                    stat[:, s0:s0 + 4], vecs[:, 24 + s0:28 + s0], vb16[:, 0:8][:, s0:s0 + 4], OP.subtract), inc=("s_ve", 1))
                R[("b2c", cc, hp, hh)] = c.n("s_ve")
                c.op("vector", lambda e, s0=s0: e.tensor_copy(
                    vb16[:, 8:16][:, s0:s0 + 4], stat[:, s0:s0 + 4]), inc=("s_ve", 1))
                R[("b2c2", cc, hp, hh)] = c.n("s_ve")

            def bias2_commit(hp, hh):
                c.wait("tensor", "s_ve", R[("b2c2", cc, hp, hh)])
                pb = psOb if hh == 0 else psMb
                if hh == 0 and hp >= 1:
                    c.wait("tensor", "s_ve", R[("aev", cc, hp - 1)])
                if hh == 1:
                    lastm = max(s for s in range(nseq) if s % 3 == 2)
                    if hp > 0:
                        c.wait("tensor", "s_ac", R[("kexp", cc, hp - 1, lastm)])
                    elif cc > 0:
                        c.wait("tensor", "s_ac", R[("expT", cc - 1)])
                vpair = vb16[:, :].rearrange("p (g c) -> p c g", c=8)
                for qt in range(4):
                    c.op("tensor", lambda e, pb=pb, qt=qt, hh=hh: e.transpose(
                        pb[64:66, qt * 128:(qt + 1) * 128],
                        vpair[:, hh * 4 + qt, :], idb[:, :]),
                        inc=("s_pe", 1) if qt == 3 else None)
                R[("b2T", cc, hp, hh)] = c.n("s_pe")
                c.wait("vector", "s_pe", R[("b2T", cc, hp, hh)])
                hc0 = (2 * hp + hh) * W
                c.op("vector", lambda e, hc0=hc0, pb=pb: e.tensor_copy(
                    qsep[64:66, hc0:hc0 + W], pb[64:66, 0:512]), inc=("s_ve", 1))
                R[("b2r", cc, hp, hh)] = c.n("s_ve")
                if cc == 0 and hp == 0 and hh == 1:
                    dump_tile(c, "vecs0", vecs[:, 0:32], 0, 128, "s_ve", R[("b2r", cc, hp, hh)])

            psT3 = [psT[0], psT[1], psM]
            psT3b = [psT[0][:, :].bitcast(BF16), psT[1][:, :].bitcast(BF16), psMb]

            def kq_geom(seq):
                hh, ik = seq // nk, seq % nk
                kq = kqs[ik]
                qs = 0 if kq < 4 else 128 * (kq - 4)
                return hh, ik, kq, qs

            def km_unit(hp, seq):
                hh, ik, kq, qs = kq_geom(seq)
                hc0 = (2 * hp + hh) * W
                i3 = seq % 3
                kbf = kaug[prv] if kq < 4 else kaug[cur]
                if seq == 0:
                    c.wait("tensor", "s_ve", R[("b2r", cc, hp, 0)])
                if seq == nk:
                    c.wait("tensor", "s_ve", R[("b2r", cc, hp, 1)])
                if seq == 2:
                    c.wait("tensor", "s_ve", R[("b2r", cc, hp, 1)])
                if seq >= 3:
                    c.wait("tensor", "s_ac", R[("kexp", cc, hp, seq - 3)])
                elif hp > 0:
                    c.wait("tensor", "s_ac", R[("kexp", cc, hp - 1, nseq - 3 + seq)])
                elif cc > 0 and seq == 0:
                    c.wait("tensor", "s_ve", R[("xr", cc - 1, 3)])
                c.op("tensor", lambda e, i3=i3, kq=kq, hc0=hc0, qs=qs, kbf=kbf: e.matmul(
                    psT3[i3][:, qs:512],
                    kbf[0:128, hc0 + (kq % 4) * 128:hc0 + (kq % 4 + 1) * 128],
                    qsep[0:128, hc0 + qs:hc0 + W],
                    start=True, stop=True), inc=("s_pe", 1))
                R[("km", cc, hp, seq)] = c.n("s_pe")
                c.wait("vector", "s_pe", R[("km", cc, hp, seq)])
                if kq >= 4:
                    c.op("vector", lambda e, i3=i3, qs=qs: e.tensor_tensor(
                        psT3[i3][:, qs:qs + 128], psT3[i3][:, qs:qs + 128], mk[:, :], OP.add),
                        inc=("s_ve", 1))
                    R[("kmask", cc, hp, seq)] = c.n("s_ve")
                    c.wait("scalar", "s_ve", R[("kmask", cc, hp, seq)])
                else:
                    c.wait("scalar", "s_pe", R[("km", cc, hp, seq)])
                if seq >= 3:
                    c.wait("scalar", "s_pe", R[("av", cc, hp, seq - 3)])
                c.op("scalar", lambda e, i3=i3, qs=qs: e.activation(
                    attnT[:, i3 * W + qs:(i3 + 1) * W], psT3[i3][:, qs:512], AF.Exp),
                    inc=("s_ac", 1))
                R[("kexp", cc, hp, seq)] = c.n("s_ac")

            def av_unit(hp, seq):
                hh, ik, kq, qs = kq_geom(seq)
                i3 = seq % 3
                c.wait("tensor", "s_ac", R[("kexp", cc, hp, seq)])
                vbf = vbuf[prv] if kq < 4 else vbuf[cur]
                c.op("tensor", lambda e, hh=hh, kq=kq, hp=hp, i3=i3, qs=qs, vbf=vbf,
                     first=(ik == 0), last=(ik == nk - 1): e.matmul(
                    psO[hh * 64:(hh + 1) * 64, qs:512],
                    vbf[:, (kq % 4) * D + (2 * hp + hh) * 64:(kq % 4) * D + (2 * hp + hh + 1) * 64],
                    attnT[:, i3 * W + qs:(i3 + 1) * W],
                    start=first, stop=last), inc=("s_pe", 1))
                R[("av", cc, hp, seq)] = c.n("s_pe")

            def kside_unit(hp, seq):
                km_unit(hp, seq)
                if seq >= 2:
                    av_unit(hp, seq - 2)
                if seq == nseq - 1:
                    av_unit(hp, nseq - 2)
                    av_unit(hp, nseq - 1)

            def aev_unit(hp):
                c.wait("vector", "s_pe", R[("av", cc, hp, nseq - 1)])
                c.op("vector", lambda e, hp=hp: e.tensor_copy(
                    aout[:, hp * W:(hp + 1) * W], psO[:, :]), inc=("s_ve", 1))
                R[("aev", cc, hp)] = c.n("s_ve")

            for hp in range(KC + 1):
                ku = ([lambda s=s, hp=hp: kside_unit(hp - 1, s) for s in range(nseq)]
                      + [lambda hp=hp: aev_unit(hp - 1)]) if hp > 0 else []
                qu = []
                if hp < KC:
                    qu = ([lambda s=s, hp=hp: qside_unit(hp, s) for s in range(4)]
                          + [lambda hp=hp: bias2_chain(hp, 0)]
                          + [lambda s=s, hp=hp: qside_unit(hp, s) for s in range(4, 8)]
                          + [lambda hp=hp: bias2_chain(hp, 1)])
                nku, nqu = len(ku), len(qu)
                ki = qi = 0
                for t in range(nku + nqu):
                    if qi * max(nku, 1) <= ki * max(nqu, 1) and qi < nqu:
                        qu[qi](); qi += 1
                    elif ki < nku:
                        ku[ki](); ki += 1
                    else:
                        qu[qi](); qi += 1
                if hp < KC:
                    bias2_commit(hp, 0)
                    bias2_commit(hp, 1)
            R[("expT", cc)] = c.n("s_ac")
        a5_block(C - 1)
        R["attn_done_ve"] = c.n("s_ve")

    def walkA_dump_x1(c):
        R = c.R
        for tt in range(NT):
            c.wait("scalar", "s_ve", R[("xr", tt // 4, tt % 4)])
            c.op("scalar", lambda e, tt=tt: e.dma_start(
                out=dump_d[tt * 128:(tt + 1) * 128, :], in_=xres[:, tt * D:(tt + 1) * D]),
                inc=("s_out", 16))

    def emitA():
        with nc.Block(no_gpsimd_drain=True) as blk:
            for name in ENGINES:
                def body(e, name=name):
                    ctxs[name].e = e
                    walkA(ctxs[name])
                    if dump is not None and dump[0] == "x1":
                        walkA_dump_x1(ctxs[name])
                getattr(blk, name)(body)

    def emit0():
        with nc.Block(no_gpsimd_drain=True) as blk:
            for name in ENGINES:
                def body(e, name=name):
                    ctxs[name].e = e
                    walk0(ctxs[name])
                getattr(blk, name)(body)

    emit0()
    emitA()
    esA.close()
    es0.close()
    esP.close()

    esB = ExitStack()
    sB = lambda n, sh, dt: esB.enter_context(nc.sbuf_tensor(n, sh, dt))
    t2fm = sB("t2fm", [128, KC * 1024], BF16)
    acc = sB("acc", [128, 8 * D], F32)
    he = sB("he", [128, HC * 1024], BF16)
    w2b = sB("w2b", [128, HC * D], BF16)
    w1g = [sB(f"w1g{i}", [128, KC * 256], BF16) for i in range(2)]
    w1l = [sB(f"w1l{i}", [128, KC * 256], BF16) for i in range(2)]
    u2b = sB("u2b", [128, D], F32)
    gfm = sB("gfm", [128, 8 * 128], F32)
    wgate = sB("wgate", [128, 8 * E], F32)
    wfmT = sB("wfmT", [8, 8 * 128], BF16)

    ppB = lambda n, sh, dt: esB.enter_context(nc.psum_tensor(n, sh, dt))
    psA_ = ppB("psA", [128, 1024], F32)
    psB_ = ppB("psB", [128, 1024], F32)
    psY = [ppB("psY0", [128, 1024], F32), ppB("psY1", [128, 1024], F32)]
    pairs = [(psA_, psB_), (psY[0], psY[1])]

    def walkB(c):
        R = c.R
        for h in range(2):
            def b1_stats(h):
                if h > 0:
                    c.wait("vector", "s_pe", R[("m2mm", h - 1, E - 1, 5)])
                for i in range(8):
                    tt = 8 * h + i
                    c.op("vector", lambda e, tt=tt: e.bn_stats(stat[:, 0:6], xres[:, tt * D:tt * D + 512]))
                    c.op("vector", lambda e, tt=tt: e.bn_stats(stat[:, 6:12], xres[:, tt * D + 512:(tt + 1) * D]))
                    c.op("vector", lambda e, i=i: e.bn_aggr(vecs[:, 2 * i:2 * i + 2], stat[:, 0:12]))
                vpx = vecs[:, 0:16].rearrange("p (a b) -> p a b", b=2)
                c.op("vector", lambda e: e.tensor_scalar(
                    vecs[:, 16:24], vpx[:, :, 1], EPS, None, OP.add), inc=("s_ve", 1))
                c.wait("scalar", "s_ve", c.n("s_ve"))
                c.op("scalar", lambda e: e.activation(vecs[:, 16:24], vecs[:, 16:24], AF.Ln))
                c.op("scalar", lambda e: e.activation(vecs[:, 16:24], vecs[:, 16:24], AF.Exp, scale=-0.5),
                     inc=("s_ac", 1))
                R[("b1inv", h)] = c.n("s_ac")

            if h == 0:
                b1_stats(0)
            c.wait("vector", "s_ac", R[("b1inv", h)])
            for i in range(8):
                tt = 8 * h + i
                c.op("vector", lambda e, tt=tt, i=i: e.tensor_scalar(
                    acc[:, i * D:(i + 1) * D], xres[:, tt * D:(tt + 1) * D],
                    vecs[:, 2 * i:2 * i + 1], vecs[:, 16 + i:17 + i], OP.subtract, OP.mult))
                c.op("vector", lambda e, i=i: e.scalar_tensor_tensor(
                    u2b[:, :], acc[:, i * D:(i + 1) * D], 1.0, acc[:, i * D:(i + 1) * D],
                    OP.mult, OP.mult, accum_out=vecs[:, 24 + i:25 + i]))
            c.op("vector", lambda e: e.tensor_scalar(
                vecs[:, 32:40], vecs[:, 24:32], 1.0 / D, EPS, OP.mult, OP.add), inc=("s_ve", 1))
            c.wait("scalar", "s_ve", c.n("s_ve"))
            c.op("scalar", lambda e: e.activation(vecs[:, 32:40], vecs[:, 32:40], AF.Ln))
            c.op("scalar", lambda e: e.activation(vecs[:, 32:40], vecs[:, 32:40], AF.Exp, scale=-0.5),
                 inc=("s_ac", 1))
            R[("rinv", h)] = c.n("s_ac")
            c.wait("vector", "s_ac", R[("rinv", h)])
            for i in range(8):
                ub = u2b[:, :]
                c.wait("vector", "s_pe", R.get(("t2T", h, i - 1, KC - 1), 0))
                c.op("vector", lambda e, i=i, ub=ub: e.tensor_scalar(
                    ub, acc[:, i * D:(i + 1) * D], vecs[:, 32 + i:33 + i], None, OP.mult),
                    inc=("s_ve", 1))
                R[("t2tm", h, i)] = c.n("s_ve")
                c.wait("tensor", "s_ve", R[("t2tm", h, i)])
                for kc in range(KC):
                    pf = psY[0][:, (kc % 2) * 512:(kc % 2) * 512 + 128]
                    if kc >= 2:
                        c.wait("tensor", "s_ve", R[("t2ev", h, i, kc - 2)])
                    elif i > 0:
                        c.wait("tensor", "s_ve", R[("t2ev", h, i - 1, kc + 6)])
                    elif h > 0:
                        c.wait("tensor", "s_ve", R[("m2ev", h - 1, E - 1, 7)])
                    c.op("tensor", lambda e, kc=kc, pf=pf, ub=ub: e.transpose(
                        pf, ub[:, kc * 128:(kc + 1) * 128], idf[:, :]), inc=("s_pe", 1))
                    R[("t2T", h, i, kc)] = c.n("s_pe")
                    c.wait("vector", "s_pe", R[("t2T", h, i, kc)])
                    c.op("vector", lambda e, kc=kc, i=i, pf=pf: e.tensor_scalar(
                        t2fm[:, kc * 1024 + i * 128:kc * 1024 + (i + 1) * 128],
                        pf, nst[:, kc:kc + 1], None, OP.mult))
                    c.op("vector", lambda e, kc=kc, pf=pf: e.tensor_scalar(
                        gfm[:, kc * 128:(kc + 1) * 128],
                        pf, nst[:, kc:kc + 1], None, OP.mult), inc=("s_ve", 1))
                    R[("t2ev", h, i, kc)] = c.n("s_ve")
                c.wait("tensor", "s_ve", R[("t2ev", h, i, KC - 1)])
                if i > 0:
                    c.wait("tensor", "s_ve", R[("wg1", h, i - 1)])
                elif h > 0:
                    c.wait("tensor", "s_ve", R[("m2ev", h - 1, E - 1, 7)])
                for kc in range(KC):
                    c.op("tensor", lambda e, kc=kc: e.matmul(
                        psY[1][:, 0:8], gfm[:, kc * 128:(kc + 1) * 128],
                        gwt[:, kc * E:(kc + 1) * E],
                        start=(kc == 0), stop=(kc == KC - 1)),
                        inc=("s_pe", 1) if kc == KC - 1 else None)
                R[("gmm", h, i)] = c.n("s_pe")
                c.wait("vector", "s_pe", R[("gmm", h, i)])
                c.op("vector", lambda e, i=i: e.tensor_tensor(
                    gl[:, i * 8:(i + 1) * 8], psY[1][:, 0:8], gbt[:, :], OP.add),
                    inc=("s_ve", 1))
                R[("wg1", h, i)] = c.n("s_ve")
            glv = gl[:, :].rearrange("p (a b) -> p a b", b=8)
            c.op("vector", lambda e: e.tensor_reduce(vecs[:, 64:72], glv, AX.X, OP.max))
            for i in range(8):
                c.op("vector", lambda e, i=i: e.tensor_scalar(
                    gm[:, i * 8:(i + 1) * 8], gl[:, i * 8:(i + 1) * 8],
                    vecs[:, 64 + i:65 + i], None, OP.is_ge))
                c.op("vector", lambda e, i=i: e.scalar_tensor_tensor(
                    gl[:, i * 8:(i + 1) * 8], gm[:, i * 8:(i + 1) * 8], NEG,
                    gl[:, i * 8:(i + 1) * 8], OP.mult, OP.add))
            c.op("vector", lambda e: e.tensor_reduce(vecs[:, 72:80], glv, AX.X, OP.max))
            c.op("vector", lambda e: e.tensor_tensor(
                vecs[:, 40:48], vecs[:, 72:80], vecs[:, 64:72], OP.subtract), inc=("s_ve", 1))
            c.wait("scalar", "s_ve", c.n("s_ve"))
            c.op("scalar", lambda e: e.activation(vecs[:, 40:48], vecs[:, 40:48], AF.Exp),
                 inc=("s_ac", 1))
            c.wait("vector", "s_ac", c.n("s_ac"))
            c.op("vector", lambda e: e.tensor_scalar(vecs[:, 40:48], vecs[:, 40:48], 1.0, None, OP.add))
            c.op("vector", lambda e: e.reciprocal(vecs[:, 48:56], vecs[:, 40:48]))
            c.op("vector", lambda e: e.tensor_scalar(vecs[:, 56:64], vecs[:, 48:56], -1.0, 1.0, OP.mult, OP.add))
            for i in range(8):
                c.op("vector", lambda e, i=i: e.tensor_scalar(
                    stat[:, 0:8], gl[:, i * 8:(i + 1) * 8], vecs[:, 72 + i:73 + i], None, OP.is_ge))
                c.op("vector", lambda e, i=i: e.tensor_scalar(
                    gm[:, i * 8:(i + 1) * 8], gm[:, i * 8:(i + 1) * 8], vecs[:, 48 + i:49 + i], None, OP.mult))
                c.op("vector", lambda e, i=i: e.scalar_tensor_tensor(
                    wgate[:, i * E:(i + 1) * E], stat[:, 0:8], vecs[:, 56 + i:57 + i],
                    gm[:, i * 8:(i + 1) * 8], OP.mult, OP.add), inc=("s_ve", 1))
                R[("wg", h, i)] = c.n("s_ve")
            for i in range(8):
                c.wait("tensor", "s_ve", R[("wg", h, i)])
                if i >= 2:
                    c.wait("tensor", "s_ve", R[("wfm", h, i - 2)])
                slot = psY[i % 2][0:8, 512:640]
                c.op("tensor", lambda e, i=i, slot=slot: e.transpose(
                    slot, wgate[:, i * E:(i + 1) * E], idf[:, :]), inc=("s_pe", 1))
                R[("wgT", h, i)] = c.n("s_pe")
                c.wait("vector", "s_pe", R[("wgT", h, i)])
                c.op("vector", lambda e, i=i, slot=slot: e.tensor_copy(
                    wfmT[:, i * 128:(i + 1) * 128], slot), inc=("s_ve", 1))
                R[("wfm", h, i)] = c.n("s_ve")
            for i in range(8):
                py = psY[i % 2]
                c.wait("tensor", "s_ve", R[("wfm", h, 7)])
                if i >= 2:
                    c.wait("tensor", "s_ve", R[("accb", h, i - 2)])
                for t in range(2):
                    c.op("tensor", lambda e, i=i, t=t, py=py: e.matmul(
                        py[:, t * 512:(t + 1) * 512], wfmT[:, i * 128:(i + 1) * 128],
                        b2t[:, t * 512:(t + 1) * 512], start=True, stop=True),
                        inc=("s_pe", 1) if t == 1 else None)
                R[("b2mm", h, i)] = c.n("s_pe")
                c.wait("vector", "s_pe", R[("b2mm", h, i)])
                c.op("vector", lambda e, i=i, py=py: e.tensor_tensor(
                    acc[:, i * D:(i + 1) * D], acc[:, i * D:(i + 1) * D], py[:, :], OP.add),
                    inc=("s_ve", 1))
                R[("accb", h, i)] = c.n("s_ve")
            if h == 0:
                dump_tile(c, "t2fm", t2fm[:, 0:KC * 1024], 0, 128, "s_ve", R[("t2ev", 0, 7, 7)])
                dump_tile(c, "wgate", wgate[:, 0:8 * E], 0, 128, "s_ve", R[("wg", 0, 7)])

            def load_piece(c, h, gq):
                e2, p2 = gq // NP, gq % NP
                buf = gq % 2
                if gq >= 2:
                    c.wait("gpsimd", "s_pe", R[("m1mm", h, (gq - 2) // NP, ((gq - 2) % NP) * 2 + 1)])
                elif h > 0:
                    c.wait("gpsimd", "s_pe", R[("m1mm", h - 1, E - 1, (NP - 2 + gq) * 2 + 1)])
                bsem = ("s_w1a", "s_w1b")[buf]
                for (dst, src_) in ((w1g, wglu_d), (w1l, wlin_d)):
                    c.op("gpsimd", lambda e, dst=dst, src_=src_, e2=e2, p2=p2, buf=buf: e.dma_start(
                        out=dst[buf][:, :], in_=src_[e2, p2, :, :]), inc=(bsem, 16))
                R[("w1ld", h, e2, p2)] = (bsem, c.n(bsem))

            for e_ in range(E):
                if e_ == 0 and h == 0:
                    c.wait("gpsimd", "s_ld", R["ld0"])
                if e_ > 0 or h > 0:
                    pe_, ph_ = (e_ - 1, h) if e_ > 0 else (E - 1, h - 1)
                    c.wait("gpsimd", "s_pe", R[("m2mm", ph_, pe_, 7)])
                for part in range(2):
                    c.op("gpsimd", lambda e, e_=e_, part=part: e.dma_start(
                        out=w2b[:, part * 8 * D:(part + 1) * 8 * D].rearrange("p (hc n) -> p hc n", n=D),
                        in_=w2_d[e_, part * 1024:(part + 1) * 1024, :].rearrange("(hc p) n -> p hc n", p=128)),
                        inc=("s_w2", 16))
                R[("w2ld", h, e_)] = c.n("s_w2")
                if e_ == 0:
                    load_piece(c, h, 0)
                    load_piece(c, h, 1)

                def m1_unit(e_, u, mm_only=False, sw_only=False):
                    p, hc = u // 2, u % 2
                    hg = u
                    buf = (e_ * NP + p) % 2
                    pr = pairs[u % 2]
                    wsem1, wval1 = R[("w1ld", h, e_, p)]
                    if not sw_only:
                        c.wait("tensor", wsem1, wval1)
                    if u == 1 and e_ == 0:
                        c.wait("tensor", "s_ve", R[("accb", h, 7)])
                    if u >= 2:
                        c.wait("tensor", "s_ac", R[("silu", h, e_, u - 2)])
                        c.wait("tensor", "s_ve", R[("sw", h, e_, u - 2)])
                    elif e_ > 0:
                        c.wait("tensor", "s_ve", R[("sw", h, e_ - 1, HC - 2 + u)])
                        if u == 0:
                            c.wait("tensor", "s_ve", R[("m2ev", h, e_ - 1, 0)])
                        if u == 1:
                            c.wait("tensor", "s_ve", R[("m2ev", h, e_ - 1, 7)])
                    elif h > 0:
                        c.wait("tensor", "s_ve", R[("sw", h - 1, E - 1, HC - 2 + u)])
                        if u == 0:
                            c.wait("tensor", "s_ve", R[("m2ev", h - 1, E - 1, 0)])
                        if u == 1:
                            c.wait("tensor", "s_ve", R[("m2ev", h - 1, E - 1, 7)])
                    elif u == 1:
                        c.wait("tensor", "s_ve", R[("accb", h, 7)])
                    if not sw_only:
                        for t in range(2):
                            if e_ == 0 and u < 4:
                                c.wait("tensor", "s_ve", R[("t2ev", h, 4 * t + 3, KC - 1)])
                            for (wsrc, pdst) in ((w1g, pr[0]), (w1l, pr[1])):
                                for kc in range(KC):
                                    c.op("tensor", lambda e, wsrc=wsrc, pdst=pdst, kc=kc, t=t, hc=hc, buf=buf: e.matmul(
                                        pdst[:, t * 512:(t + 1) * 512],
                                        wsrc[buf][:, kc * 256 + hc * 128:kc * 256 + (hc + 1) * 128],
                                        t2fm[:, kc * 1024 + t * 512:kc * 1024 + (t + 1) * 512],
                                        start=(kc == 0), stop=(kc == KC - 1)),
                                        inc=("s_pe", 1) if (kc == KC - 1 and t == 1) else None)
                        R[("m1mm", h, e_, u)] = c.n("s_pe")
                    if mm_only:
                        return
                    c.wait("scalar", "s_pe", R[("m1mm", h, e_, u)] - 1)
                    pv = (R.get(("sw", h, e_, u - 1)) if u > 0 else
                          (R.get(("sw", h, e_ - 1, HC - 1)) if e_ > 0 else
                           (R.get(("sw", h - 1, E - 1, HC - 1)) if h > 0 else None)))
                    if pv:
                        c.wait("scalar", "s_ve", pv)
                    c.op("scalar", lambda e, pr=pr, e_=e_, hg=hg: e.activation(
                        u2b[:, :], pr[0][:, :], AF.Silu,
                        bias=b1gt[:, e_ * HC + hg:e_ * HC + hg + 1], scale=ALPHA), inc=("s_ac", 1))
                    R[("silu", h, e_, u)] = c.n("s_ac")
                    c.wait("vector", "s_ac", R[("silu", h, e_, u)])
                    c.op("vector", lambda e: e.tensor_scalar(
                        u2b[:, :], u2b[:, :], CSAT, 1.0 / ALPHA, OP.min, OP.mult))
                    c.wait("vector", "s_pe", R[("m1mm", h, e_, u)])
                    c.op("vector", lambda e, pr=pr, e_=e_, hg=hg: e.tensor_scalar(
                        pr[1][:, :], pr[1][:, :], b1lt[:, e_ * HC + hg:e_ * HC + hg + 1], LIMIT,
                        OP.add, OP.min))
                    c.op("vector", lambda e, pr=pr: e.tensor_scalar(
                        pr[1][:, :], pr[1][:, :], -LIMIT, 1.0, OP.max, OP.add))
                    if sw_only:
                        c.wait("vector", "s_pe", R[("m2mm", h, e_ - 1, 7)])
                    c.op("vector", lambda e, pr=pr, hg=hg: e.tensor_tensor(
                        he[:, hg * 1024:(hg + 1) * 1024], u2b[:, :], pr[1][:, :], OP.mult),
                        inc=("s_ve", 1))
                    R[("sw", h, e_, u)] = c.n("s_ve")
                    if u % 2 == 1:
                        gq = e_ * NP + u // 2 + 2
                        if gq < E * NP:
                            load_piece(c, h, gq)

                for u in range(1 if e_ > 0 else 0, HC):
                    m1_unit(e_, u, sw_only=(u == 0 and False))
                if h == 0 and e_ == 0:
                    dump_tile(c, "he", he[:, 0:HC * 1024], 0, 128, "s_ve", R[("sw", 0, 0, 15)])

                c.wait("tensor", "s_w2", R[("w2ld", h, e_)])
                for i in range(8):
                    py = psA_ if i == 0 else psY[i % 2]
                    if i > 0:
                        c.wait("tensor", "s_ve", R[("sw", h, e_, HC - 1)])
                    if i == 0:
                        c.wait("tensor", "s_ve", R[("sw", h, e_, HC - 2)])
                    if i >= 2:
                        c.wait("tensor", "s_ve", R[("m2ev", h, e_, i - 2)])
                    elif e_ == 0 and h == 0 and i < 2:
                        c.wait("tensor", "s_ve", R[("accb", h, 7)])
                    for hc in range(HC):
                        if i == 0:
                            c.wait("tensor", "s_ve", R[("sw", h, e_, hc)])
                        for t in range(2):
                            c.op("tensor", lambda e, py=py, hc=hc, t=t, i=i: e.matmul(
                                py[:, t * 512:(t + 1) * 512],
                                he[:, hc * 1024 + i * 128:hc * 1024 + (i + 1) * 128],
                                w2b[:, hc * D + t * 512:hc * D + (t + 1) * 512],
                                start=(hc == 0), stop=(hc == HC - 1)),
                                inc=("s_pe", 1) if (hc == HC - 1 and t == 1) else None)
                    R[("m2mm", h, e_, i)] = c.n("s_pe")
                    c.wait("vector", "s_pe", R[("m2mm", h, e_, i)])
                    c.op("vector", lambda e, py=py, i=i, e_=e_: e.scalar_tensor_tensor(
                        acc[:, i * D:(i + 1) * D], py[:, :], wgate[:, i * E + e_:i * E + e_ + 1],
                        acc[:, i * D:(i + 1) * D], OP.mult, OP.add), inc=("s_ve", 1))
                    R[("m2ev", h, e_, i)] = c.n("s_ve")
                    if i == 6 and e_ + 1 < E:
                        m1_unit(e_ + 1, 0, mm_only=True)
                    if i == 7 and e_ + 1 < E:
                        m1_unit(e_ + 1, 0, sw_only=True)
            if h == 0:
                dump_tile(c, "acc", acc[:, 0:8 * D], 0, 128, "s_ve", R[("m2ev", 0, 7, 7)])

            if h == 0:
                b1_stats(1)
            for i in range(8):
                c.wait("vector", "s_ve", R[("m2ev", h, E - 1, i)])
                c.op("vector", lambda e, i=i: e.bn_stats(stat[:, 0:6], acc[:, i * D:i * D + 512]))
                c.op("vector", lambda e, i=i: e.bn_stats(stat[:, 6:12], acc[:, i * D + 512:(i + 1) * D]))
                c.op("vector", lambda e, i=i: e.bn_aggr(vecs[:, 48 + 2 * i:50 + 2 * i], stat[:, 0:12]))
            vpz = vecs[:, 48:64].rearrange("p (a b) -> p a b", b=2)
            c.op("vector", lambda e: e.tensor_scalar(
                vecs[:, 64:72], vpz[:, :, 1], EPS, None, OP.add), inc=("s_ve", 1))
            c.wait("scalar", "s_ve", c.n("s_ve"))
            c.op("scalar", lambda e: e.activation(vecs[:, 64:72], vecs[:, 64:72], AF.Ln))
            c.op("scalar", lambda e: e.activation(vecs[:, 64:72], vecs[:, 64:72], AF.Exp, scale=-0.5),
                 inc=("s_ac", 1))
            c.wait("vector", "s_ac", c.n("s_ac"))
            for i in range(8):
                tt = 8 * h + i
                c.op("vector", lambda e, i=i: e.tensor_scalar(
                    u2b[:, :], acc[:, i * D:(i + 1) * D], vecs[:, 48 + 2 * i:49 + 2 * i],
                    vecs[:, 64 + i:65 + i], OP.subtract, OP.mult))
                c.op("vector", lambda e, tt=tt: e.tensor_tensor(
                    xres[:, tt * D:(tt + 1) * D], xres[:, tt * D:(tt + 1) * D], u2b[:, :], OP.add),
                    inc=("s_ve", 1))
                R[("xout", h, i)] = c.n("s_ve")
                c.wait("scalar", "s_ve", R[("xout", h, i)])
                c.op("scalar", lambda e, tt=tt: e.dma_start(
                    out=y_d[tt * 128:(tt + 1) * 128, :], in_=xres[:, tt * D:(tt + 1) * D]),
                    inc=("s_out", 16))
        c.wait("scalar", "s_out", c.n("s_out"))

    with nc.Block(no_gpsimd_drain=True) as blk:
        for name in ENGINES:
            def body(e, name=name):
                ctxs[name].e = e
                walkB(ctxs[name])
            getattr(blk, name)(body)

    esB.close()
    es.close()
    return nc


def build_nc_fenced(dump=None, race_debug=False):
    collect = {}
    build_nc(dump=dump, race_debug=False, _collect=collect)
    fences = {e: _fence_plan(seq) for e, seq in collect.items()}
    return build_nc(dump=dump, race_debug=race_debug, _fences=fences)


_CACHE = {}


def _host_layout(L0_attn_x, L0_attn_w_qkv, L0_attn_w_out, L0_ffn_norm_scale,
                 L0_ffn_gate_w, L0_ffn_gate_b, L0_ffn_mlp1_w, L0_ffn_mlp1_b,
                 L0_ffn_mlp2_w, L0_ffn_mlp2_b):
    f = np.ascontiguousarray
    x = np.asarray(L0_attn_x, dtype=np.float32)
    shared = {
        "wqkv": f(np.asarray(L0_attn_w_qkv, np.float32)),
        "wout": f(np.asarray(L0_attn_w_out, np.float32)),
        "ns": f(np.asarray(L0_ffn_norm_scale, np.float32).reshape(KC, 128).T),
        "gatew": f(np.asarray(L0_ffn_gate_w, np.float32).reshape(KC, 128, E)
                   .transpose(1, 0, 2).reshape(128, KC * E)),
        "gateb": f(np.tile(np.asarray(L0_ffn_gate_b, np.float32)[None, :], (128, 1))),
        "wglu": f(np.asarray(L0_ffn_mlp1_w, np.float32)[:, :, 0::2]
                  .reshape(E, KC, 128, NP, 256).transpose(0, 3, 2, 1, 4)
                  .reshape(E, NP, 128, KC * 256)),
        "wlin": f(np.asarray(L0_ffn_mlp1_w, np.float32)[:, :, 1::2]
                  .reshape(E, KC, 128, NP, 256).transpose(0, 3, 2, 1, 4)
                  .reshape(E, NP, 128, KC * 256)),
        "b1g": f(np.asarray(L0_ffn_mlp1_b, np.float32)[:, 0::2].reshape(E, HC, 128)
                 .transpose(0, 2, 1)),
        "b1l": f(np.asarray(L0_ffn_mlp1_b, np.float32)[:, 1::2].reshape(E, HC, 128)
                 .transpose(0, 2, 1)),
        "w2": f(np.asarray(L0_ffn_mlp2_w, np.float32)),
        "b2": f(np.asarray(L0_ffn_mlp2_b, np.float32)),
    }
    return x, shared


def kernel(**inputs):
    x, shared = _host_layout(**inputs)
    if "nc" not in _CACHE:
        _CACHE["nc"] = build_nc_fenced()
    nc = _CACHE["nc"]
    in_maps = [dict(shared, x=np.ascontiguousarray(x[b])) for b in range(8)]
    res = run_bass_kernel_spmd(nc, in_maps, core_ids=list(range(8)))
    return np.stack([np.asarray(res.results[b]["y"], np.float32) for b in range(8)], axis=0)


# version 25
# speedup vs baseline: 1.1101x; 1.1101x over previous
import numpy as np
import concourse.bass as bass
import concourse.mybir as mybir
from concourse.bass_utils import run_bass_kernel_spmd
from concourse.alu_op_type import AluOpType as OP
from contextlib import ExitStack

F32 = mybir.dt.float32
BF16 = mybir.dt.bfloat16
AF = mybir.ActivationFunctionType
AX = mybir.AxisListType

S, D, H, HD, W = 2048, 1024, 16, 64, 512
C = S // W
NT = S // 128
KC = D // 128
E, IH = 8, 2048
HC = IH // 128
NP = 8
EPS, ALPHA, LIMIT, NEG = 1e-5, 1.702, 7.0, -1.0e30
CSAT = float(ALPHA * LIMIT / (1.0 + np.exp(-ALPHA * LIMIT)))
ENGINES = ("sync", "gpsimd", "vector", "scalar", "tensor")


_DT_SIZE = {F32: 4, BF16: 2}
_FENCED = ("vector", "scalar", "gpsimd")


def _footprints(inst):
    nm = type(inst).__name__
    if "DMA" in nm or "Dma" in nm:
        return [], []

    def rng(arg):
        if not hasattr(arg, "ap") or not hasattr(arg, "memref"):
            return None
        sz = _DT_SIZE.get(arg.dtype, 4)
        pairs = list(arg.ap)
        hi = arg.offset
        for (st, cnt) in pairs[1:]:
            hi += abs(int(st)) * (int(cnt) - 1)
        return (arg.memref, int(arg.offset) * sz, (hi + 1) * sz)

    rds = [r for r in (rng(a) for a in inst.ins) if r is not None]
    wrs = [r for r in (rng(a) for a in inst.outs) if r is not None]
    return rds, wrs


def _overlap(aa, bb):
    for (n1, l1, h1) in aa:
        for (n2, l2, h2) in bb:
            if n1 == n2 and l1 < h2 and l2 < h1:
                return True
    return False


def _fence_plan(seq):
    fences = set()
    pw, pr = [], []
    for i, (rd, wr) in enumerate(seq):
        if _overlap(rd, pw) or _overlap(wr, pw) or _overlap(wr, pr):
            fences.add(i)
            pw, pr = [], []
        pw += wr
        pr += rd
    return fences


class Ctx:

    def __init__(self, name, sems, collect=None, fences=None):
        self.a = name
        self.e = None
        self.cnt = {k: 0 for k in sems}
        self.sems = sems
        self.R = {}
        self.collect = collect
        self.fences = fences if fences is not None else set()
        self.idx = 0

    def op(self, eng, fn, inc=None):
        incs = [] if inc is None else (inc if isinstance(inc, list) else [inc])
        instr = None
        if eng == self.a:
            if eng in _FENCED:
                if self.idx in self.fences:
                    self.e.drain()
                self.idx += 1
            instr = fn(self.e)
            if self.collect is not None and eng in _FENCED:
                self.collect.append(_footprints(instr.ins))
        for (sn, k) in incs:
            self.cnt[sn] += k
            if instr is not None:
                instr.then_inc(self.sems[sn], k)

    def wait(self, eng, sn, val):
        if val and val > 0 and eng == self.a:
            self.e.wait_ge(self.sems[sn], val)

    def n(self, sn):
        return self.cnt[sn]


def build_nc(dump=None, race_debug=False, _fences=None, _collect=None):
    nc = bass.Bass("TRN2", target_bir_lowering=False, debug=False,
                   detect_race_conditions=bool(race_debug))

    x_d = nc.dram_tensor("x", [S, D], F32, kind="ExternalInput")
    wqkv_d = nc.dram_tensor("wqkv", [D, 3 * D], F32, kind="ExternalInput")
    wout_d = nc.dram_tensor("wout", [D, D], F32, kind="ExternalInput")
    ns_d = nc.dram_tensor("ns", [128, KC], F32, kind="ExternalInput")
    gw_d = nc.dram_tensor("gatew", [128, KC * E], F32, kind="ExternalInput")
    gb_d = nc.dram_tensor("gateb", [128, E], F32, kind="ExternalInput")
    wglu_d = nc.dram_tensor("wglu", [E, NP, 128, KC * 256], F32, kind="ExternalInput")
    wlin_d = nc.dram_tensor("wlin", [E, NP, 128, KC * 256], F32, kind="ExternalInput")
    b1g_d = nc.dram_tensor("b1g", [E, 128, HC], F32, kind="ExternalInput")
    b1l_d = nc.dram_tensor("b1l", [E, 128, HC], F32, kind="ExternalInput")
    w2_d = nc.dram_tensor("w2", [E, IH, D], F32, kind="ExternalInput")
    b2_d = nc.dram_tensor("b2", [E, D], F32, kind="ExternalInput")
    y_d = nc.dram_tensor("y", [S, D], F32, kind="ExternalOutput")
    dump_d = None
    if dump is not None:
        dump_d = nc.dram_tensor("dbg", list(dump[1]), dump[2], kind="ExternalOutput")

    es = ExitStack()
    sb = lambda n, sh, dt: es.enter_context(nc.sbuf_tensor(n, sh, dt))

    xres = sb("xres", [128, NT * D], F32)
    nst = sb("nst", [128, KC], F32)
    gwt = sb("gwt", [128, KC * E], F32)
    gbt = sb("gbt", [128, E], F32)
    b1gt = sb("b1gt", [128, E * HC], F32)
    b1lt = sb("b1lt", [128, E * HC], F32)
    b2t = sb("b2t", [E, D], BF16)
    idf = sb("idf", [128, 128], F32)
    stat = sb("stat", [128, 16], F32)
    vecs = sb("vecs", [128, 96], F32)
    vb16 = sb("vb16", [128, 16], BF16)
    gl = sb("gl", [128, 64], F32)
    gm = sb("gm", [128, 64], F32)

    sems = {k: es.enter_context(nc.semaphore(k))
            for k in ("s_ld", "s_x", "s_gw", "s_pe", "s_ve", "s_ac", "s_gp", "s_out",
                      "s_qk0", "s_qk1", "s_w1a", "s_w1b", "s_w2")}
    ctxs = {name: Ctx(name, sems,
                      collect=(_collect.setdefault(name, []) if _collect is not None else None),
                      fences=(_fences or {}).get(name))
            for name in ENGINES}

    def dump_tile(c, key, src_ap, row0, nrow, wsem=None, wval=0):
        if dump is None or dump[0] != key:
            return
        if wsem is not None:
            c.wait("scalar", wsem, wval)
        c.op("scalar", lambda e: e.dma_start(
            out=dump_d[row0:row0 + nrow, :], in_=src_ap), inc=("s_out", 16))
        for eng in ("vector", "scalar", "tensor", "gpsimd"):
            c.wait(eng, "s_out", c.n("s_out"))

    es0 = ExitStack()
    scr = es0.enter_context(nc.sbuf_tensor("scr0", [128, 64], F32))
    scrb = es0.enter_context(nc.sbuf_tensor("scr0b", [128, 64], BF16))

    esP = ExitStack()
    pp = lambda n, sh, dt: esP.enter_context(nc.psum_tensor(n, sh, dt))
    psS = [None, None]
    psS[0] = pp("psSa", [128, 1024], F32)
    psS[1] = pp("psSb", [128, 1024], F32)
    psT = [pp("psT0", [128, 512], F32), pp("psT1", [128, 512], F32)]
    psO = pp("psO", [128, 512], F32)
    psM = pp("psM", [128, 512], F32)
    psMb = psM[:, :].bitcast(BF16)
    psOb = psO[:, :].bitcast(BF16)

    ps0 = psS[0][:, 0:512]

    esA = ExitStack()
    sA = lambda n, sh, dt: esA.enter_context(nc.sbuf_tensor(n, sh, dt))
    idb = sA("idb", [128, 128], BF16)
    mq = sA("mq", [128, 128], F32)
    mk = sA("mk", [128, 128], F32)
    onesr = sA("onesr", [1, 128], F32)
    irow = sA("irow", [128, 128], F32)
    icol = sA("icol", [128, 1], F32)
    wvb = sA("wvb", [128, KC * D], BF16)
    wqk = [sA(f"wqk{i}", [128, KC * 256], BF16) for i in range(2)]
    woutb = sA("woutb", [128, KC * D], BF16)
    t1tm = sA("t1tm", [128, 2 * D], BF16)
    t1fm = sA("t1fm", [128, KC * W], BF16)
    qsep = sA("qsep", [128, 16 * W], BF16)
    kaug = [sA(f"kaug{i}", [128, 16 * W], BF16) for i in range(2)]
    vbuf = [sA(f"vbuf{i}", [128, 4 * D], BF16) for i in range(2)]
    qexp = sA("qexp", [128, 1024], BF16)
    attnT = sA("attnT", [128, 3 * W], BF16)
    aout = sA("aout", [128, 8 * W], BF16)
    tmpa = sA("tmpa", [128, D], F32)

    def walk0(c):
        R = c.R
        c.op("sync", lambda e: e.dma_start(
            out=xres[:, 0:4 * D].rearrange("p (t d) -> p t d", d=D),
            in_=x_d[0:W, :].rearrange("(t p) d -> p t d", p=128)), inc=("s_x", 16))
        for cc in range(1, C):
            c.op("sync", lambda e, cc=cc: e.dma_start(
                out=xres[:, cc * 4 * D:(cc + 1) * 4 * D].rearrange("p (t d) -> p t d", d=D),
                in_=x_d[cc * W:(cc + 1) * W, :].rearrange("(t p) d -> p t d", p=128)),
                inc=("s_ld", 16))
        for (dst, src) in ((nst, ns_d), (gwt, gw_d), (gbt, gb_d)):
            c.op("sync", lambda e, dst=dst, src=src: e.dma_start(out=dst[:, :], in_=src[:, :]),
                 inc=("s_ld", 16))
        c.op("sync", lambda e: e.dma_start(
            out=b1gt[:, :].rearrange("p (e h) -> p e h", e=E),
            in_=b1g_d[:, :, :].transpose((1, 0, 2))), inc=("s_ld", 16))
        c.op("sync", lambda e: e.dma_start(
            out=b1lt[:, :].rearrange("p (e h) -> p e h", e=E),
            in_=b1l_d[:, :, :].transpose((1, 0, 2))), inc=("s_ld", 16))
        R["ld0"] = c.n("s_ld")

        c.op("gpsimd", lambda e: e.iota(icol[:, :], [[0, 1]], channel_multiplier=1,
                                        allow_small_or_imprecise_dtypes=True), inc=("s_gp", 1))
        c.op("gpsimd", lambda e: e.iota(irow[:, :], [[1, 128]], channel_multiplier=0,
                                        allow_small_or_imprecise_dtypes=True), inc=("s_gp", 1))
        for kc in range(KC):
            c.op("gpsimd", lambda e, kc=kc: e.dma_start(
                out=wvb[:, kc * D:(kc + 1) * D],
                in_=wqkv_d[kc * 128:(kc + 1) * 128, 2 * D:3 * D]), inc=("s_gw", 16))
        for kc in range(KC):
            c.op("gpsimd", lambda e, kc=kc: e.dma_start(
                out=woutb[:, kc * D:(kc + 1) * D],
                in_=wout_d[kc * 128:(kc + 1) * 128, :]), inc=("s_gw", 16))
        c.op("gpsimd", lambda e: e.dma_start(out=b2t[:, :], in_=b2_d[:, :]), inc=("s_gw", 16))
        c.op("gpsimd", lambda e: e.dma_start(
            out=wqk[0][:, :].rearrange("p (k j) -> p k j", k=KC)[:, :, 0:128],
            in_=wqkv_d[:, :].rearrange("(k p) m -> p k m", p=128)[:, :, 0:128]), inc=("s_qk0", 16))
        c.op("gpsimd", lambda e: e.dma_start(
            out=wqk[0][:, :].rearrange("p (k j) -> p k j", k=KC)[:, :, 128:256],
            in_=wqkv_d[:, :].rearrange("(k p) m -> p k m", p=128)[:, :, D:D + 128]), inc=("s_qk0", 16))
        R["gw0"] = c.n("s_gw")
        R[("qkb", 0)] = ("s_qk0", c.n("s_qk0"))
        R[("qkb", 1)] = ("s_qk1", 0)

        c.wait("vector", "s_gp", 2)
        c.wait("vector", "s_ld", R["ld0"])
        c.op("vector", lambda e: e.tensor_scalar(mq[:, :], irow[:, :], icol[:, :], NEG, OP.is_gt, OP.mult))
        c.op("vector", lambda e: e.tensor_scalar(mk[:, :], irow[:, :], icol[:, :], NEG, OP.is_lt, OP.mult))
        c.op("vector", lambda e: e.tensor_scalar(idf[:, :], irow[:, :], icol[:, :], None, OP.is_equal))
        c.op("vector", lambda e: e.tensor_copy(idb[:, :], idf[:, :]))
        c.op("vector", lambda e: e.memset(onesr[:, :], 1.0))
        c.op("vector", lambda e: e.memset(qsep[64:128, :], 0.0))
        for kk in range(2):
            c.op("vector", lambda e, kk=kk: e.memset(kaug[kk][64:128, :], 0.0))
            c.op("vector", lambda e, kk=kk: e.memset(kaug[kk][64:66, :], 1.0))
        c.op("vector", lambda e: e.tensor_scalar(b1gt[:, :], b1gt[:, :], ALPHA, None, OP.mult))
        c.op("vector", lambda e: e.bn_stats(scr[:, 0:6], irow[:, 0:32]))
        c.op("vector", lambda e: e.bn_aggr(scr[:, 6:8], scr[:, 0:6]))
        c.op("vector", lambda e: e.tensor_reduce(scr[:, 8:9], irow[:, 0:32], AX.X, OP.max, negate=True))
        c.op("vector", lambda e: e.scalar_tensor_tensor(scr[:, 9:10], icol[:, :], 1.0, icol[:, :], OP.mult, OP.add))
        c.op("vector", lambda e: e.reciprocal(scr[:, 10:11], scr[:, 9:10]))
        c.op("vector", lambda e: e.tensor_copy(scrb[:, 0:32], irow[:, 0:32]), inc=("s_ve", 1))
        R["ve0"] = c.n("s_ve")

        c.wait("tensor", "s_ve", R["ve0"])
        c.op("tensor", lambda e: e.matmul(ps0[0:16, 0:16], scrb[:, 0:16], scrb[:, 16:32], start=True, stop=True))
        c.op("tensor", lambda e: e.transpose(ps0[:, 128:256], idf[:, :], idf[:, :]))
        c.op("tensor", lambda e: e.transpose(ps0[:, 256:320].bitcast(BF16), idb[:, :], idb[:, :]),
             inc=("s_pe", 1))
        R["pe0"] = c.n("s_pe")

        c.wait("scalar", "s_ve", R["ve0"])
        c.op("scalar", lambda e: e.activation(scr[:, 16:17], scr[:, 9:10], AF.Exp, scale=-0.01))
        c.op("scalar", lambda e: e.activation(scr[:, 17:18], scr[:, 16:17], AF.Ln))
        c.op("scalar", lambda e: e.activation(scr[:, 18:19], scr[:, 9:10], AF.Silu), inc=("s_ac", 1))
        c.wait("scalar", "s_pe", R["pe0"])
        c.op("scalar", lambda e: e.copy(scr[:, 19:20], ps0[:, 128:129]))

    def ln_tile(c, src_ap, wait_src=None):
        if wait_src is not None:
            for (sn, v) in wait_src:
                c.wait("vector", sn, v)
        c.op("vector", lambda e: e.bn_stats(stat[:, 0:6], src_ap[:, 0:512]))
        c.op("vector", lambda e: e.bn_stats(stat[:, 6:12], src_ap[:, 512:1024]))
        c.op("vector", lambda e: e.bn_aggr(stat[:, 12:14], stat[:, 0:12]))
        c.op("vector", lambda e: e.tensor_scalar(stat[:, 14:15], stat[:, 13:14], EPS, None, OP.add),
             inc=("s_ve", 1))
        c.wait("scalar", "s_ve", c.n("s_ve"))
        c.op("scalar", lambda e: e.activation(stat[:, 15:16], stat[:, 14:15], AF.Ln))
        c.op("scalar", lambda e: e.activation(stat[:, 15:16], stat[:, 15:16], AF.Exp, scale=-0.5),
             inc=("s_ac", 1))
        c.wait("vector", "s_ac", c.n("s_ac"))

    def walkA(c):
        R = c.R

        def a5_block(c5):
            for j in range(4):
                tt = 4 * c5 + j
                pW = psS[j % 2]
                c.wait("tensor", "s_ve", R[("aev", c5, KC - 1)])
                if j < 2:
                    c.wait("tensor", "s_ac", R[("qexp", c5, KC - 1)])
                else:
                    c.wait("tensor", "s_ve", R[("xr", c5, j - 2)])
                for dk in range(KC):
                    for t in range(2):
                        c.op("tensor", lambda e, dk=dk, t=t, j=j, pW=pW: e.matmul(
                            pW[:, t * 512:(t + 1) * 512],
                            aout[:, dk * W + j * 128:dk * W + (j + 1) * 128],
                            woutb[:, dk * D + t * 512:dk * D + (t + 1) * 512],
                            start=(dk == 0), stop=(dk == KC - 1)),
                            inc=("s_pe", 1) if (dk == KC - 1 and t == 1) else None)
                R[("wo", c5, j)] = c.n("s_pe")
                ln_tile(c, pW, wait_src=[("s_pe", R[("wo", c5, j)])])
                c.op("vector", lambda e, pW=pW: e.tensor_scalar(
                    tmpa[:, :], pW[:, :], stat[:, 12:13], stat[:, 15:16], OP.subtract, OP.mult))
                c.op("vector", lambda e, tt=tt: e.tensor_tensor(
                    xres[:, tt * D:(tt + 1) * D], xres[:, tt * D:(tt + 1) * D], tmpa[:, :], OP.add),
                    inc=("s_ve", 1))
                R[("xr", c5, j)] = c.n("s_ve")
        c.wait("gpsimd", "s_gw", 0)
        for cc in range(C):
            cur, prv = cc % 2, 1 - cc % 2
            if cc == 0:
                c.wait("vector", "s_x", 16)
                c.wait("vector", "s_gp", 2)
            elif cc == 1:
                c.wait("vector", "s_ld", R["ld0"])
            for j in range(4):
                tt = 4 * cc + j
                c.op("vector", lambda e, tt=tt: e.bn_stats(stat[:, 0:6], xres[:, tt * D:tt * D + 512]))
                c.op("vector", lambda e, tt=tt: e.bn_stats(stat[:, 6:12], xres[:, tt * D + 512:(tt + 1) * D]))
                c.op("vector", lambda e, j=j: e.bn_aggr(vecs[:, 32 + 2 * j:34 + 2 * j], stat[:, 0:12]))
            vpr = vecs[:, 32:40].rearrange("p (a b) -> p a b", b=2)
            c.op("vector", lambda e: e.tensor_scalar(
                vecs[:, 40:44], vpr[:, :, 1], EPS, None, OP.add), inc=("s_ve", 1))
            c.wait("scalar", "s_ve", c.n("s_ve"))
            c.op("scalar", lambda e: e.activation(vecs[:, 40:44], vecs[:, 40:44], AF.Ln))
            c.op("scalar", lambda e: e.activation(vecs[:, 40:44], vecs[:, 40:44], AF.Exp, scale=-0.5),
                 inc=("s_ac", 1))
            c.wait("vector", "s_ac", c.n("s_ac"))
            for j in range(4):
                tt = 4 * cc + j
                g_ = 4 * cc + j
                if g_ >= 2:
                    c.wait("vector", "s_pe", R[("t1T", (g_ - 2) // 4, (g_ - 2) % 4)])
                c.op("vector", lambda e, tt=tt, j=j: e.tensor_scalar(
                    t1tm[:, (j % 2) * D:(j % 2 + 1) * D], xres[:, tt * D:(tt + 1) * D],
                    vecs[:, 32 + 2 * j:33 + 2 * j], vecs[:, 40 + j:41 + j], OP.subtract, OP.mult), inc=("s_ve", 1))
                R[("t1", cc, j)] = c.n("s_ve")
                c.wait("tensor", "s_ve", R[("t1", cc, j)])
                if cc > 0 and j == 0:
                    c.wait("tensor", "s_ac", R[("expT", cc - 1)])
                    c.wait("tensor", "s_ve", R[("b2r", cc - 1, KC - 1, 1)])
                elif (cc, j) != (0, 0):
                    pj = (cc * 4 + j) - 1
                    c.wait("tensor", "s_ve", R[("t1fm", pj // 4, pj % 4)])
                else:
                    c.wait("tensor", "s_pe", R["pe0"])
                for kc in range(KC):
                    c.op("tensor", lambda e, j=j, kc=kc: e.transpose(
                        psMb[:, kc * 128:(kc + 1) * 128],
                        t1tm[:, (j % 2) * D + kc * 128:(j % 2) * D + (kc + 1) * 128], idb[:, :]),
                        inc=("s_pe", 1) if kc == KC - 1 else None)
                R[("t1T", cc, j)] = c.n("s_pe")
                c.wait("vector", "s_pe", R[("t1T", cc, j)])
                c.op("vector", lambda e, j=j: e.tensor_copy(
                    t1fm[:, :].rearrange("p (k w) -> p k w", k=KC)[:, :, j * 128:(j + 1) * 128],
                    psMb[:, :].rearrange("p (k t) -> p k t", k=KC)), inc=("s_ve", 1))
                R[("t1fm", cc, j)] = c.n("s_ve")
            if cc == 0:
                dump_tile(c, "t1fm", t1fm[:, 0:KC * W], 0, 128, "s_ve", R[("t1fm", 0, 3)])

            if cc > 0:
                a5_block(cc - 1)
            c.wait("tensor", "s_ve", R[("t1fm", cc, 3)])
            c.wait("tensor", "s_gw", R["gw0"])
            if cc > 0:
                c.wait("tensor", "s_ve", R[("xr", cc - 1, 3)])
                c.wait("tensor", "s_ac", R[("expT", cc - 1)])
            for j in range(4):
                if j >= 2:
                    c.wait("tensor", "s_ve", R[("vev", cc, j - 2)])
                for kc in range(KC):
                    for t in range(2):
                        c.op("tensor", lambda e, j=j, kc=kc, t=t: e.matmul(
                            psS[j % 2][:, t * 512:(t + 1) * 512],
                            t1fm[:, kc * W + j * 128:kc * W + (j + 1) * 128],
                            wvb[:, kc * D + t * 512:kc * D + (t + 1) * 512],
                            start=(kc == 0), stop=(kc == KC - 1)),
                            inc=("s_pe", 1) if (kc == KC - 1 and t == 1) else None)
                R[("vmm", cc, j)] = c.n("s_pe")
                c.wait("vector", "s_pe", R[("vmm", cc, j)])
                c.op("vector", lambda e, j=j, cur=cur: e.tensor_copy(
                    vbuf[cur][:, j * D:(j + 1) * D], psS[j % 2][:, :]), inc=("s_ve", 1))
                R[("vev", cc, j)] = c.n("s_ve")
            for hp in range(KC):
                buf = hp % 2
                nxt = hp + 1
                if nxt < KC or cc < C - 1:
                    nhp = nxt % KC
                    nsem = "s_qk%d" % (1 - buf)
                    if hp >= 1:
                        c.wait("gpsimd", "s_pe", R[("qk_mm", cc, hp - 1)])
                    elif cc > 0:
                        c.wait("gpsimd", "s_pe", R[("qk_mm", cc - 1, KC - 1)])
                    c.op("gpsimd", lambda e, nhp=nhp: e.dma_start(
                        out=wqk[1 - buf][:, :].rearrange("p (k j) -> p k j", k=KC)[:, :, 0:128],
                        in_=wqkv_d[:, :].rearrange("(k p) m -> p k m", p=128)[:, :, nhp * 128:(nhp + 1) * 128]),
                        inc=(nsem, 16))
                    c.op("gpsimd", lambda e, nhp=nhp: e.dma_start(
                        out=wqk[1 - buf][:, :].rearrange("p (k j) -> p k j", k=KC)[:, :, 128:256],
                        in_=wqkv_d[:, :].rearrange("(k p) m -> p k m", p=128)[:, :, D + nhp * 128:D + (nhp + 1) * 128]),
                        inc=(nsem, 16))
                    R[("qkb", 1 - buf)] = (nsem, c.n(nsem))
                wsem, wval = R[("qkb", buf)]
                c.wait("tensor", wsem, wval)
                if hp >= 1:
                    c.wait("tensor", "s_ve", R[("kev", cc, hp - 1)])
                elif cc > 0:
                    c.wait("tensor", "s_ve", R[("kev", cc - 1, KC - 1)])
                pt_q, pt_k = psT[hp % 2], psT[1 - hp % 2]
                for (qk, pt) in ((0, pt_q), (1, pt_k)):
                    for kc in range(KC):
                        c.op("tensor", lambda e, qk=qk, pt=pt, kc=kc, buf=buf: e.matmul(
                            pt[:, :],
                            wqk[buf][:, kc * 256 + qk * 128:kc * 256 + (qk + 1) * 128],
                            t1fm[:, kc * W:(kc + 1) * W],
                            start=(kc == 0), stop=(kc == KC - 1)),
                            inc=("s_pe", 1) if kc == KC - 1 else None)
                R[("qk_mm", cc, hp)] = c.n("s_pe")
                c.wait("vector", "s_pe", R[("qk_mm", cc, hp)])
                for hh in range(2):
                    c.op("vector", lambda e, hp=hp, hh=hh, pt_q=pt_q: e.tensor_scalar(
                        qsep[0:64, (2 * hp + hh) * W:(2 * hp + hh + 1) * W],
                        pt_q[hh * 64:hh * 64 + 64, :], 1.0 / 8.0, None, OP.mult),
                        inc=("s_ve", 1))
                    c.op("vector", lambda e, hp=hp, hh=hh, cur=cur, pt_k=pt_k: e.tensor_copy(
                        kaug[cur][0:64, (2 * hp + hh) * W:(2 * hp + hh + 1) * W],
                        pt_k[hh * 64:hh * 64 + 64, :]), inc=("s_ve", 1))
                R[("kev", cc, hp)] = c.n("s_ve")
            if cc == 0:
                dump_tile(c, "qkv_q", qsep[0:64, 0:8 * W], 0, 64, "s_ve", R[("kev", 0, 7)])
                dump_tile(c, "qkv_k", kaug[cur][0:64, 0:8 * W], 0, 64, "s_ve", R[("kev", 0, 7)])
                dump_tile(c, "qkv_v", vbuf[cur][:, 0:4 * D], 0, 128, "s_ve", R[("kev", 0, 7)])

            klo = 0 if cc > 0 else 512
            kqs = list(range(8)) if cc > 0 else list(range(4, 8))
            nk = len(kqs)
            nseq = 2 * nk

            def qside_unit(hp, qseq):
                hh, qt = qseq // 4, qseq % 4
                hc0 = (2 * hp + hh) * W
                pS = psS[qseq % 2]
                if qseq >= 2:
                    c.wait("tensor", "s_ac", R[("qexpi", cc, hp, qseq - 2)])
                elif hp == 0:
                    c.wait("tensor", "s_ve", R[("kev", cc, 0)])
                else:
                    c.wait("tensor", "s_ac", R[("qexpi", cc, hp - 1, 6 + qseq)])
                if cc > 0:
                    c.op("tensor", lambda e, hc0=hc0, qt=qt, pS=pS, prv=prv: e.matmul(
                        pS[:, 0:512],
                        qsep[0:64, hc0 + qt * 128:hc0 + (qt + 1) * 128],
                        kaug[prv][0:64, hc0:hc0 + W],
                        start=True, stop=True))
                c.op("tensor", lambda e, hc0=hc0, qt=qt, pS=pS, cur=cur: e.matmul(
                    pS[:, 512:512 + 128 * (qt + 1)],
                    qsep[0:64, hc0 + qt * 128:hc0 + (qt + 1) * 128],
                    kaug[cur][0:64, hc0:hc0 + 128 * (qt + 1)],
                    start=True, stop=True), inc=("s_pe", 1))
                R[("qmi", cc, hp, qseq)] = c.n("s_pe")
                c.wait("vector", "s_pe", R[("qmi", cc, hp, qseq)])
                if hp > 0 and qseq == 0:
                    c.wait("vector", "s_ac", R[("qexp", cc, hp - 1)])
                c.op("vector", lambda e, pS=pS, qt=qt: e.tensor_tensor(
                    pS[:, 512 + qt * 128:512 + (qt + 1) * 128],
                    pS[:, 512 + qt * 128:512 + (qt + 1) * 128], mq[:, :], OP.add))
                c.op("vector", lambda e, pS=pS, qt=qt, hh=hh: e.tensor_reduce(
                    vecs[:, hh * 4 + qt:hh * 4 + qt + 1],
                    pS[:, klo:512 + 128 * (qt + 1)], AX.X, OP.max, negate=True),
                    inc=("s_ve", 1))
                R[("qred", cc, hp, qseq)] = c.n("s_ve")
                c.wait("scalar", "s_ve", R[("qred", cc, hp, qseq)])
                c.op("scalar", lambda e, pS=pS, qt=qt, hh=hh: e.activation(
                    qexp[:, 0:512 + 128 * (qt + 1) - klo],
                    pS[:, klo:512 + 128 * (qt + 1)], AF.Exp,
                    bias=vecs[:, hh * 4 + qt:hh * 4 + qt + 1],
                    accum_out=vecs[:, 8 + hh * 4 + qt:8 + hh * 4 + qt + 1]),
                    inc=("s_ac", 1))
                R[("qexpi", cc, hp, qseq)] = c.n("s_ac")
                if qseq == 7:
                    R[("qexp", cc, hp)] = c.n("s_ac")

            def bias2_chain(hp, hh):
                s0 = hh * 4
                if hp > 0:
                    c.wait("scalar", "s_ve", R[("b2c", cc, hp - 1, hh)])
                c.op("scalar", lambda e, s0=s0: e.activation(
                    vecs[:, 16 + s0:20 + s0], vecs[:, 8 + s0:12 + s0], AF.Ln), inc=("s_ac", 1))
                c.wait("vector", "s_ac", c.n("s_ac"))
                if hp > 0:
                    c.wait("vector", "s_pe", R[("b2T", cc, hp - 1, hh)])
                c.op("vector", lambda e, s0=s0: e.tensor_tensor(
                    vecs[:, 24 + s0:28 + s0], vecs[:, s0:4 + s0], vecs[:, 16 + s0:20 + s0], OP.subtract))
                c.op("vector", lambda e, s0=s0: e.tensor_copy(
                    vb16[:, 0:8][:, s0:s0 + 4], vecs[:, 24 + s0:28 + s0]))
                c.op("vector", lambda e, s0=s0: e.tensor_tensor(
                    stat[:, s0:s0 + 4], vecs[:, 24 + s0:28 + s0], vb16[:, 0:8][:, s0:s0 + 4], OP.subtract), inc=("s_ve", 1))
                R[("b2c", cc, hp, hh)] = c.n("s_ve")
                c.op("vector", lambda e, s0=s0: e.tensor_copy(
                    vb16[:, 8:16][:, s0:s0 + 4], stat[:, s0:s0 + 4]), inc=("s_ve", 1))
                R[("b2c2", cc, hp, hh)] = c.n("s_ve")

            def bias2_commit(hp, hh):
                c.wait("tensor", "s_ve", R[("b2c2", cc, hp, hh)])
                pb = psOb if hh == 0 else psMb
                if hh == 0 and hp >= 1:
                    c.wait("tensor", "s_ve", R[("aev", cc, hp - 1)])
                if hh == 1:
                    lastm = max(s for s in range(nseq) if s % 3 == 2)
                    if hp > 0:
                        c.wait("tensor", "s_ac", R[("kexp", cc, hp - 1, lastm)])
                    elif cc > 0:
                        c.wait("tensor", "s_ac", R[("expT", cc - 1)])
                vpair = vb16[:, :].rearrange("p (g c) -> p c g", c=8)
                for qt in range(4):
                    c.op("tensor", lambda e, pb=pb, qt=qt, hh=hh: e.transpose(
                        pb[64:66, qt * 128:(qt + 1) * 128],
                        vpair[:, hh * 4 + qt, :], idb[:, :]),
                        inc=("s_pe", 1) if qt == 3 else None)
                R[("b2T", cc, hp, hh)] = c.n("s_pe")
                c.wait("vector", "s_pe", R[("b2T", cc, hp, hh)])
                hc0 = (2 * hp + hh) * W
                c.op("vector", lambda e, hc0=hc0, pb=pb: e.tensor_copy(
                    qsep[64:66, hc0:hc0 + W], pb[64:66, 0:512]), inc=("s_ve", 1))
                R[("b2r", cc, hp, hh)] = c.n("s_ve")
                if cc == 0 and hp == 0 and hh == 1:
                    dump_tile(c, "vecs0", vecs[:, 0:32], 0, 128, "s_ve", R[("b2r", cc, hp, hh)])

            psT3 = [psT[0], psT[1], psM]
            psT3b = [psT[0][:, :].bitcast(BF16), psT[1][:, :].bitcast(BF16), psMb]

            def kq_geom(seq):
                hh, ik = seq // nk, seq % nk
                kq = kqs[ik]
                qs = 0 if kq < 4 else 128 * (kq - 4)
                return hh, ik, kq, qs

            def km_unit(hp, seq):
                hh, ik, kq, qs = kq_geom(seq)
                hc0 = (2 * hp + hh) * W
                i3 = seq % 3
                kbf = kaug[prv] if kq < 4 else kaug[cur]
                if seq == 0:
                    c.wait("tensor", "s_ve", R[("b2r", cc, hp, 0)])
                if seq == nk:
                    c.wait("tensor", "s_ve", R[("b2r", cc, hp, 1)])
                if seq == 2:
                    c.wait("tensor", "s_ve", R[("b2r", cc, hp, 1)])
                if seq >= 3:
                    c.wait("tensor", "s_ac", R[("kexp", cc, hp, seq - 3)])
                elif hp > 0:
                    c.wait("tensor", "s_ac", R[("kexp", cc, hp - 1, nseq - 3 + seq)])
                elif cc > 0 and seq == 0:
                    c.wait("tensor", "s_ve", R[("xr", cc - 1, 3)])
                c.op("tensor", lambda e, i3=i3, kq=kq, hc0=hc0, qs=qs, kbf=kbf: e.matmul(
                    psT3[i3][:, qs:512],
                    kbf[0:128, hc0 + (kq % 4) * 128:hc0 + (kq % 4 + 1) * 128],
                    qsep[0:128, hc0 + qs:hc0 + W],
                    start=True, stop=True), inc=("s_pe", 1))
                R[("km", cc, hp, seq)] = c.n("s_pe")
                c.wait("vector", "s_pe", R[("km", cc, hp, seq)])
                if kq >= 4:
                    c.op("vector", lambda e, i3=i3, qs=qs: e.tensor_tensor(
                        psT3[i3][:, qs:qs + 128], psT3[i3][:, qs:qs + 128], mk[:, :], OP.add),
                        inc=("s_ve", 1))
                    R[("kmask", cc, hp, seq)] = c.n("s_ve")
                    c.wait("scalar", "s_ve", R[("kmask", cc, hp, seq)])
                else:
                    c.wait("scalar", "s_pe", R[("km", cc, hp, seq)])
                if seq >= 3:
                    c.wait("scalar", "s_pe", R[("av", cc, hp, seq - 3)])
                c.op("scalar", lambda e, i3=i3, qs=qs: e.activation(
                    attnT[:, i3 * W + qs:(i3 + 1) * W], psT3[i3][:, qs:512], AF.Exp),
                    inc=("s_ac", 1))
                R[("kexp", cc, hp, seq)] = c.n("s_ac")

            def av_unit(hp, seq):
                hh, ik, kq, qs = kq_geom(seq)
                i3 = seq % 3
                c.wait("tensor", "s_ac", R[("kexp", cc, hp, seq)])
                vbf = vbuf[prv] if kq < 4 else vbuf[cur]
                c.op("tensor", lambda e, hh=hh, kq=kq, hp=hp, i3=i3, qs=qs, vbf=vbf,
                     first=(ik == 0), last=(ik == nk - 1): e.matmul(
                    psO[hh * 64:(hh + 1) * 64, qs:512],
                    vbf[:, (kq % 4) * D + (2 * hp + hh) * 64:(kq % 4) * D + (2 * hp + hh + 1) * 64],
                    attnT[:, i3 * W + qs:(i3 + 1) * W],
                    start=first, stop=last), inc=("s_pe", 1))
                R[("av", cc, hp, seq)] = c.n("s_pe")

            def kside_unit(hp, seq):
                km_unit(hp, seq)
                if seq >= 2:
                    av_unit(hp, seq - 2)
                if seq == nseq - 1:
                    av_unit(hp, nseq - 2)
                    av_unit(hp, nseq - 1)

            def aev_unit(hp):
                c.wait("vector", "s_pe", R[("av", cc, hp, nseq - 1)])
                c.op("vector", lambda e, hp=hp: e.tensor_copy(
                    aout[:, hp * W:(hp + 1) * W], psO[:, :]), inc=("s_ve", 1))
                R[("aev", cc, hp)] = c.n("s_ve")

            for hp in range(KC + 1):
                ku = ([lambda s=s, hp=hp: kside_unit(hp - 1, s) for s in range(nseq)]
                      + [lambda hp=hp: aev_unit(hp - 1)]) if hp > 0 else []
                qu = []
                if hp < KC:
                    qu = ([lambda s=s, hp=hp: qside_unit(hp, s) for s in range(4)]
                          + [lambda hp=hp: bias2_chain(hp, 0)]
                          + [lambda s=s, hp=hp: qside_unit(hp, s) for s in range(4, 8)]
                          + [lambda hp=hp: bias2_chain(hp, 1)])
                nku, nqu = len(ku), len(qu)
                ki = qi = 0
                for t in range(nku + nqu):
                    if ki * max(nqu, 1) <= qi * max(nku, 1) and ki < nku:
                        ku[ki](); ki += 1
                    elif qi < nqu:
                        qu[qi](); qi += 1
                    else:
                        ku[ki](); ki += 1
                if hp < KC:
                    bias2_commit(hp, 0)
                    bias2_commit(hp, 1)
            R[("expT", cc)] = c.n("s_ac")
        a5_block(C - 1)
        R["attn_done_ve"] = c.n("s_ve")

    def walkA_dump_x1(c):
        R = c.R
        for tt in range(NT):
            c.wait("scalar", "s_ve", R[("xr", tt // 4, tt % 4)])
            c.op("scalar", lambda e, tt=tt: e.dma_start(
                out=dump_d[tt * 128:(tt + 1) * 128, :], in_=xres[:, tt * D:(tt + 1) * D]),
                inc=("s_out", 16))

    def emitA():
        with nc.Block(no_gpsimd_drain=True) as blk:
            for name in ENGINES:
                def body(e, name=name):
                    ctxs[name].e = e
                    walkA(ctxs[name])
                    if dump is not None and dump[0] == "x1":
                        walkA_dump_x1(ctxs[name])
                getattr(blk, name)(body)

    def emit0():
        with nc.Block(no_gpsimd_drain=True) as blk:
            for name in ENGINES:
                def body(e, name=name):
                    ctxs[name].e = e
                    walk0(ctxs[name])
                getattr(blk, name)(body)

    emit0()
    emitA()
    esA.close()
    es0.close()
    esP.close()

    esB = ExitStack()
    sB = lambda n, sh, dt: esB.enter_context(nc.sbuf_tensor(n, sh, dt))
    t2fm = sB("t2fm", [128, KC * 1024], BF16)
    acc = sB("acc", [128, 8 * D], F32)
    he = sB("he", [128, HC * 1024], BF16)
    w2b = sB("w2b", [128, HC * D], BF16)
    w1g = [sB(f"w1g{i}", [128, KC * 256], BF16) for i in range(2)]
    w1l = [sB(f"w1l{i}", [128, KC * 256], BF16) for i in range(2)]
    u2b = sB("u2b", [128, D], F32)
    gfm = sB("gfm", [128, 8 * 128], F32)
    wgate = sB("wgate", [128, 8 * E], F32)
    wfmT = sB("wfmT", [8, 8 * 128], BF16)

    ppB = lambda n, sh, dt: esB.enter_context(nc.psum_tensor(n, sh, dt))
    psA_ = ppB("psA", [128, 1024], F32)
    psB_ = ppB("psB", [128, 1024], F32)
    psY = [ppB("psY0", [128, 1024], F32), ppB("psY1", [128, 1024], F32)]
    pairs = [(psA_, psB_), (psY[0], psY[1])]

    def walkB(c):
        R = c.R
        for h in range(2):
            def b1_stats(h):
                if h > 0:
                    c.wait("vector", "s_pe", R[("m2mm", h - 1, E - 1, 5)])
                for i in range(8):
                    tt = 8 * h + i
                    c.op("vector", lambda e, tt=tt: e.bn_stats(stat[:, 0:6], xres[:, tt * D:tt * D + 512]))
                    c.op("vector", lambda e, tt=tt: e.bn_stats(stat[:, 6:12], xres[:, tt * D + 512:(tt + 1) * D]))
                    c.op("vector", lambda e, i=i: e.bn_aggr(vecs[:, 2 * i:2 * i + 2], stat[:, 0:12]))
                vpx = vecs[:, 0:16].rearrange("p (a b) -> p a b", b=2)
                c.op("vector", lambda e: e.tensor_scalar(
                    vecs[:, 16:24], vpx[:, :, 1], EPS, None, OP.add), inc=("s_ve", 1))
                c.wait("scalar", "s_ve", c.n("s_ve"))
                c.op("scalar", lambda e: e.activation(vecs[:, 16:24], vecs[:, 16:24], AF.Ln))
                c.op("scalar", lambda e: e.activation(vecs[:, 16:24], vecs[:, 16:24], AF.Exp, scale=-0.5),
                     inc=("s_ac", 1))
                R[("b1inv", h)] = c.n("s_ac")

            if h == 0:
                b1_stats(0)
            c.wait("vector", "s_ac", R[("b1inv", h)])
            for i in range(8):
                tt = 8 * h + i
                c.op("vector", lambda e, tt=tt, i=i: e.tensor_scalar(
                    acc[:, i * D:(i + 1) * D], xres[:, tt * D:(tt + 1) * D],
                    vecs[:, 2 * i:2 * i + 1], vecs[:, 16 + i:17 + i], OP.subtract, OP.mult))
                c.op("vector", lambda e, i=i: e.scalar_tensor_tensor(
                    u2b[:, :], acc[:, i * D:(i + 1) * D], 1.0, acc[:, i * D:(i + 1) * D],
                    OP.mult, OP.mult, accum_out=vecs[:, 24 + i:25 + i]))
            c.op("vector", lambda e: e.tensor_scalar(
                vecs[:, 32:40], vecs[:, 24:32], 1.0 / D, EPS, OP.mult, OP.add), inc=("s_ve", 1))
            c.wait("scalar", "s_ve", c.n("s_ve"))
            c.op("scalar", lambda e: e.activation(vecs[:, 32:40], vecs[:, 32:40], AF.Ln))
            c.op("scalar", lambda e: e.activation(vecs[:, 32:40], vecs[:, 32:40], AF.Exp, scale=-0.5),
                 inc=("s_ac", 1))
            R[("rinv", h)] = c.n("s_ac")
            c.wait("vector", "s_ac", R[("rinv", h)])
            for i in range(8):
                ub = u2b[:, :]
                c.wait("vector", "s_pe", R.get(("t2T", h, i - 1, KC - 1), 0))
                c.op("vector", lambda e, i=i, ub=ub: e.tensor_scalar(
                    ub, acc[:, i * D:(i + 1) * D], vecs[:, 32 + i:33 + i], None, OP.mult),
                    inc=("s_ve", 1))
                R[("t2tm", h, i)] = c.n("s_ve")
                c.wait("tensor", "s_ve", R[("t2tm", h, i)])
                for kc in range(KC):
                    pf = psY[0][:, (kc % 2) * 512:(kc % 2) * 512 + 128]
                    if kc >= 2:
                        c.wait("tensor", "s_ve", R[("t2ev", h, i, kc - 2)])
                    elif i > 0:
                        c.wait("tensor", "s_ve", R[("t2ev", h, i - 1, kc + 6)])
                    elif h > 0:
                        c.wait("tensor", "s_ve", R[("m2ev", h - 1, E - 1, 7)])
                    c.op("tensor", lambda e, kc=kc, pf=pf, ub=ub: e.transpose(
                        pf, ub[:, kc * 128:(kc + 1) * 128], idf[:, :]), inc=("s_pe", 1))
                    R[("t2T", h, i, kc)] = c.n("s_pe")
                    c.wait("vector", "s_pe", R[("t2T", h, i, kc)])
                    c.op("vector", lambda e, kc=kc, i=i, pf=pf: e.tensor_scalar(
                        t2fm[:, kc * 1024 + i * 128:kc * 1024 + (i + 1) * 128],
                        pf, nst[:, kc:kc + 1], None, OP.mult))
                    c.op("vector", lambda e, kc=kc, pf=pf: e.tensor_scalar(
                        gfm[:, kc * 128:(kc + 1) * 128],
                        pf, nst[:, kc:kc + 1], None, OP.mult), inc=("s_ve", 1))
                    R[("t2ev", h, i, kc)] = c.n("s_ve")
                c.wait("tensor", "s_ve", R[("t2ev", h, i, KC - 1)])
                if i > 0:
                    c.wait("tensor", "s_ve", R[("wg1", h, i - 1)])
                elif h > 0:
                    c.wait("tensor", "s_ve", R[("m2ev", h - 1, E - 1, 7)])
                for kc in range(KC):
                    c.op("tensor", lambda e, kc=kc: e.matmul(
                        psY[1][:, 0:8], gfm[:, kc * 128:(kc + 1) * 128],
                        gwt[:, kc * E:(kc + 1) * E],
                        start=(kc == 0), stop=(kc == KC - 1)),
                        inc=("s_pe", 1) if kc == KC - 1 else None)
                R[("gmm", h, i)] = c.n("s_pe")
                c.wait("vector", "s_pe", R[("gmm", h, i)])
                c.op("vector", lambda e, i=i: e.tensor_tensor(
                    gl[:, i * 8:(i + 1) * 8], psY[1][:, 0:8], gbt[:, :], OP.add),
                    inc=("s_ve", 1))
                R[("wg1", h, i)] = c.n("s_ve")
            glv = gl[:, :].rearrange("p (a b) -> p a b", b=8)
            c.op("vector", lambda e: e.tensor_reduce(vecs[:, 64:72], glv, AX.X, OP.max))
            for i in range(8):
                c.op("vector", lambda e, i=i: e.tensor_scalar(
                    gm[:, i * 8:(i + 1) * 8], gl[:, i * 8:(i + 1) * 8],
                    vecs[:, 64 + i:65 + i], None, OP.is_ge))
                c.op("vector", lambda e, i=i: e.scalar_tensor_tensor(
                    gl[:, i * 8:(i + 1) * 8], gm[:, i * 8:(i + 1) * 8], NEG,
                    gl[:, i * 8:(i + 1) * 8], OP.mult, OP.add))
            c.op("vector", lambda e: e.tensor_reduce(vecs[:, 72:80], glv, AX.X, OP.max))
            c.op("vector", lambda e: e.tensor_tensor(
                vecs[:, 40:48], vecs[:, 72:80], vecs[:, 64:72], OP.subtract), inc=("s_ve", 1))
            c.wait("scalar", "s_ve", c.n("s_ve"))
            c.op("scalar", lambda e: e.activation(vecs[:, 40:48], vecs[:, 40:48], AF.Exp),
                 inc=("s_ac", 1))
            c.wait("vector", "s_ac", c.n("s_ac"))
            c.op("vector", lambda e: e.tensor_scalar(vecs[:, 40:48], vecs[:, 40:48], 1.0, None, OP.add))
            c.op("vector", lambda e: e.reciprocal(vecs[:, 48:56], vecs[:, 40:48]))
            c.op("vector", lambda e: e.tensor_scalar(vecs[:, 56:64], vecs[:, 48:56], -1.0, 1.0, OP.mult, OP.add))
            for i in range(8):
                c.op("vector", lambda e, i=i: e.tensor_scalar(
                    stat[:, 0:8], gl[:, i * 8:(i + 1) * 8], vecs[:, 72 + i:73 + i], None, OP.is_ge))
                c.op("vector", lambda e, i=i: e.tensor_scalar(
                    gm[:, i * 8:(i + 1) * 8], gm[:, i * 8:(i + 1) * 8], vecs[:, 48 + i:49 + i], None, OP.mult))
                c.op("vector", lambda e, i=i: e.scalar_tensor_tensor(
                    wgate[:, i * E:(i + 1) * E], stat[:, 0:8], vecs[:, 56 + i:57 + i],
                    gm[:, i * 8:(i + 1) * 8], OP.mult, OP.add), inc=("s_ve", 1))
                R[("wg", h, i)] = c.n("s_ve")
            for i in range(8):
                c.wait("tensor", "s_ve", R[("wg", h, i)])
                if i >= 2:
                    c.wait("tensor", "s_ve", R[("wfm", h, i - 2)])
                slot = psY[i % 2][0:8, 512:640]
                c.op("tensor", lambda e, i=i, slot=slot: e.transpose(
                    slot, wgate[:, i * E:(i + 1) * E], idf[:, :]), inc=("s_pe", 1))
                R[("wgT", h, i)] = c.n("s_pe")
                c.wait("vector", "s_pe", R[("wgT", h, i)])
                c.op("vector", lambda e, i=i, slot=slot: e.tensor_copy(
                    wfmT[:, i * 128:(i + 1) * 128], slot), inc=("s_ve", 1))
                R[("wfm", h, i)] = c.n("s_ve")
            for i in range(8):
                py = psY[i % 2]
                c.wait("tensor", "s_ve", R[("wfm", h, 7)])
                if i >= 2:
                    c.wait("tensor", "s_ve", R[("accb", h, i - 2)])
                for t in range(2):
                    c.op("tensor", lambda e, i=i, t=t, py=py: e.matmul(
                        py[:, t * 512:(t + 1) * 512], wfmT[:, i * 128:(i + 1) * 128],
                        b2t[:, t * 512:(t + 1) * 512], start=True, stop=True),
                        inc=("s_pe", 1) if t == 1 else None)
                R[("b2mm", h, i)] = c.n("s_pe")
                c.wait("vector", "s_pe", R[("b2mm", h, i)])
                c.op("vector", lambda e, i=i, py=py: e.tensor_tensor(
                    acc[:, i * D:(i + 1) * D], acc[:, i * D:(i + 1) * D], py[:, :], OP.add),
                    inc=("s_ve", 1))
                R[("accb", h, i)] = c.n("s_ve")
            if h == 0:
                dump_tile(c, "t2fm", t2fm[:, 0:KC * 1024], 0, 128, "s_ve", R[("t2ev", 0, 7, 7)])
                dump_tile(c, "wgate", wgate[:, 0:8 * E], 0, 128, "s_ve", R[("wg", 0, 7)])

            def load_piece(c, h, gq):
                e2, p2 = gq // NP, gq % NP
                buf = gq % 2
                if gq >= 2:
                    c.wait("gpsimd", "s_pe", R[("m1mm", h, (gq - 2) // NP, ((gq - 2) % NP) * 2 + 1)])
                elif h > 0:
                    c.wait("gpsimd", "s_pe", R[("m1mm", h - 1, E - 1, (NP - 2 + gq) * 2 + 1)])
                bsem = ("s_w1a", "s_w1b")[buf]
                for (dst, src_) in ((w1g, wglu_d), (w1l, wlin_d)):
                    c.op("gpsimd", lambda e, dst=dst, src_=src_, e2=e2, p2=p2, buf=buf: e.dma_start(
                        out=dst[buf][:, :], in_=src_[e2, p2, :, :]), inc=(bsem, 16))
                R[("w1ld", h, e2, p2)] = (bsem, c.n(bsem))

            for e_ in range(E):
                if e_ == 0 and h == 0:
                    c.wait("gpsimd", "s_ld", R["ld0"])
                if e_ > 0 or h > 0:
                    pe_, ph_ = (e_ - 1, h) if e_ > 0 else (E - 1, h - 1)
                    c.wait("gpsimd", "s_pe", R[("m2mm", ph_, pe_, 7)])
                for part in range(2):
                    c.op("gpsimd", lambda e, e_=e_, part=part: e.dma_start(
                        out=w2b[:, part * 8 * D:(part + 1) * 8 * D].rearrange("p (hc n) -> p hc n", n=D),
                        in_=w2_d[e_, part * 1024:(part + 1) * 1024, :].rearrange("(hc p) n -> p hc n", p=128)),
                        inc=("s_w2", 16))
                R[("w2ld", h, e_)] = c.n("s_w2")
                if e_ == 0:
                    load_piece(c, h, 0)
                    load_piece(c, h, 1)

                def m1_unit(e_, u, mm_only=False, sw_only=False):
                    p, hc = u // 2, u % 2
                    hg = u
                    buf = (e_ * NP + p) % 2
                    pr = pairs[u % 2]
                    wsem1, wval1 = R[("w1ld", h, e_, p)]
                    if not sw_only:
                        c.wait("tensor", wsem1, wval1)
                    if u == 1 and e_ == 0:
                        c.wait("tensor", "s_ve", R[("accb", h, 7)])
                    if u >= 2:
                        c.wait("tensor", "s_ac", R[("silu", h, e_, u - 2)])
                        c.wait("tensor", "s_ve", R[("sw", h, e_, u - 2)])
                    elif e_ > 0:
                        c.wait("tensor", "s_ve", R[("sw", h, e_ - 1, HC - 2 + u)])
                        if u == 0:
                            c.wait("tensor", "s_ve", R[("m2ev", h, e_ - 1, 0)])
                        if u == 1:
                            c.wait("tensor", "s_ve", R[("m2ev", h, e_ - 1, 7)])
                    elif h > 0:
                        c.wait("tensor", "s_ve", R[("sw", h - 1, E - 1, HC - 2 + u)])
                        if u == 0:
                            c.wait("tensor", "s_ve", R[("m2ev", h - 1, E - 1, 0)])
                        if u == 1:
                            c.wait("tensor", "s_ve", R[("m2ev", h - 1, E - 1, 7)])
                    elif u == 1:
                        c.wait("tensor", "s_ve", R[("accb", h, 7)])
                    if not sw_only:
                        for t in range(2):
                            if e_ == 0 and u < 4:
                                c.wait("tensor", "s_ve", R[("t2ev", h, 4 * t + 3, KC - 1)])
                            for (wsrc, pdst) in ((w1g, pr[0]), (w1l, pr[1])):
                                for kc in range(KC):
                                    c.op("tensor", lambda e, wsrc=wsrc, pdst=pdst, kc=kc, t=t, hc=hc, buf=buf: e.matmul(
                                        pdst[:, t * 512:(t + 1) * 512],
                                        wsrc[buf][:, kc * 256 + hc * 128:kc * 256 + (hc + 1) * 128],
                                        t2fm[:, kc * 1024 + t * 512:kc * 1024 + (t + 1) * 512],
                                        start=(kc == 0), stop=(kc == KC - 1)),
                                        inc=("s_pe", 1) if (kc == KC - 1 and t == 1) else None)
                        R[("m1mm", h, e_, u)] = c.n("s_pe")
                    if mm_only:
                        return
                    c.wait("scalar", "s_pe", R[("m1mm", h, e_, u)] - 1)
                    pv = (R.get(("sw", h, e_, u - 1)) if u > 0 else
                          (R.get(("sw", h, e_ - 1, HC - 1)) if e_ > 0 else
                           (R.get(("sw", h - 1, E - 1, HC - 1)) if h > 0 else None)))
                    if pv:
                        c.wait("scalar", "s_ve", pv)
                    c.op("scalar", lambda e, pr=pr, e_=e_, hg=hg: e.activation(
                        u2b[:, :], pr[0][:, :], AF.Silu,
                        bias=b1gt[:, e_ * HC + hg:e_ * HC + hg + 1], scale=ALPHA), inc=("s_ac", 1))
                    R[("silu", h, e_, u)] = c.n("s_ac")
                    c.wait("vector", "s_ac", R[("silu", h, e_, u)])
                    c.op("vector", lambda e: e.tensor_scalar(
                        u2b[:, :], u2b[:, :], CSAT, 1.0 / ALPHA, OP.min, OP.mult))
                    c.wait("vector", "s_pe", R[("m1mm", h, e_, u)])
                    c.op("vector", lambda e, pr=pr, e_=e_, hg=hg: e.tensor_scalar(
                        pr[1][:, :], pr[1][:, :], b1lt[:, e_ * HC + hg:e_ * HC + hg + 1], LIMIT,
                        OP.add, OP.min))
                    c.op("vector", lambda e, pr=pr: e.tensor_scalar(
                        pr[1][:, :], pr[1][:, :], -LIMIT, 1.0, OP.max, OP.add))
                    if sw_only:
                        c.wait("vector", "s_pe", R[("m2mm", h, e_ - 1, 7)])
                    c.op("vector", lambda e, pr=pr, hg=hg: e.tensor_tensor(
                        he[:, hg * 1024:(hg + 1) * 1024], u2b[:, :], pr[1][:, :], OP.mult),
                        inc=("s_ve", 1))
                    R[("sw", h, e_, u)] = c.n("s_ve")
                    if u % 2 == 1:
                        gq = e_ * NP + u // 2 + 2
                        if gq < E * NP:
                            load_piece(c, h, gq)

                for u in range(1 if e_ > 0 else 0, HC):
                    m1_unit(e_, u, sw_only=(u == 0 and False))
                if h == 0 and e_ == 0:
                    dump_tile(c, "he", he[:, 0:HC * 1024], 0, 128, "s_ve", R[("sw", 0, 0, 15)])

                c.wait("tensor", "s_w2", R[("w2ld", h, e_)])
                for i in range(8):
                    py = psA_ if i == 0 else psY[i % 2]
                    if i > 0:
                        c.wait("tensor", "s_ve", R[("sw", h, e_, HC - 1)])
                    if i == 0:
                        c.wait("tensor", "s_ve", R[("sw", h, e_, HC - 2)])
                    if i >= 2:
                        c.wait("tensor", "s_ve", R[("m2ev", h, e_, i - 2)])
                    elif e_ == 0 and h == 0 and i < 2:
                        c.wait("tensor", "s_ve", R[("accb", h, 7)])
                    for hc in range(HC):
                        if i == 0:
                            c.wait("tensor", "s_ve", R[("sw", h, e_, hc)])
                        for t in range(2):
                            c.op("tensor", lambda e, py=py, hc=hc, t=t, i=i: e.matmul(
                                py[:, t * 512:(t + 1) * 512],
                                he[:, hc * 1024 + i * 128:hc * 1024 + (i + 1) * 128],
                                w2b[:, hc * D + t * 512:hc * D + (t + 1) * 512],
                                start=(hc == 0), stop=(hc == HC - 1)),
                                inc=("s_pe", 1) if (hc == HC - 1 and t == 1) else None)
                    R[("m2mm", h, e_, i)] = c.n("s_pe")
                    c.wait("vector", "s_pe", R[("m2mm", h, e_, i)])
                    c.op("vector", lambda e, py=py, i=i, e_=e_: e.scalar_tensor_tensor(
                        acc[:, i * D:(i + 1) * D], py[:, :], wgate[:, i * E + e_:i * E + e_ + 1],
                        acc[:, i * D:(i + 1) * D], OP.mult, OP.add), inc=("s_ve", 1))
                    R[("m2ev", h, e_, i)] = c.n("s_ve")
                    if i == 6 and e_ + 1 < E:
                        m1_unit(e_ + 1, 0, mm_only=True)
                    if i == 7 and e_ + 1 < E:
                        m1_unit(e_ + 1, 0, sw_only=True)
            if h == 0:
                dump_tile(c, "acc", acc[:, 0:8 * D], 0, 128, "s_ve", R[("m2ev", 0, 7, 7)])

            if h == 0:
                b1_stats(1)
            for i in range(8):
                c.wait("vector", "s_ve", R[("m2ev", h, E - 1, i)])
                c.op("vector", lambda e, i=i: e.bn_stats(stat[:, 0:6], acc[:, i * D:i * D + 512]))
                c.op("vector", lambda e, i=i: e.bn_stats(stat[:, 6:12], acc[:, i * D + 512:(i + 1) * D]))
                c.op("vector", lambda e, i=i: e.bn_aggr(vecs[:, 48 + 2 * i:50 + 2 * i], stat[:, 0:12]))
            vpz = vecs[:, 48:64].rearrange("p (a b) -> p a b", b=2)
            c.op("vector", lambda e: e.tensor_scalar(
                vecs[:, 64:72], vpz[:, :, 1], EPS, None, OP.add), inc=("s_ve", 1))
            c.wait("scalar", "s_ve", c.n("s_ve"))
            c.op("scalar", lambda e: e.activation(vecs[:, 64:72], vecs[:, 64:72], AF.Ln))
            c.op("scalar", lambda e: e.activation(vecs[:, 64:72], vecs[:, 64:72], AF.Exp, scale=-0.5),
                 inc=("s_ac", 1))
            c.wait("vector", "s_ac", c.n("s_ac"))
            for i in range(8):
                tt = 8 * h + i
                c.op("vector", lambda e, i=i: e.tensor_scalar(
                    u2b[:, :], acc[:, i * D:(i + 1) * D], vecs[:, 48 + 2 * i:49 + 2 * i],
                    vecs[:, 64 + i:65 + i], OP.subtract, OP.mult))
                c.op("vector", lambda e, tt=tt: e.tensor_tensor(
                    xres[:, tt * D:(tt + 1) * D], xres[:, tt * D:(tt + 1) * D], u2b[:, :], OP.add),
                    inc=("s_ve", 1))
                R[("xout", h, i)] = c.n("s_ve")
                c.wait("scalar", "s_ve", R[("xout", h, i)])
                c.op("scalar", lambda e, tt=tt: e.dma_start(
                    out=y_d[tt * 128:(tt + 1) * 128, :], in_=xres[:, tt * D:(tt + 1) * D]),
                    inc=("s_out", 16))
        c.wait("scalar", "s_out", c.n("s_out"))

    with nc.Block(no_gpsimd_drain=True) as blk:
        for name in ENGINES:
            def body(e, name=name):
                ctxs[name].e = e
                walkB(ctxs[name])
            getattr(blk, name)(body)

    esB.close()
    es.close()
    return nc


def build_nc_fenced(dump=None, race_debug=False):
    collect = {}
    build_nc(dump=dump, race_debug=False, _collect=collect)
    fences = {e: _fence_plan(seq) for e, seq in collect.items()}
    return build_nc(dump=dump, race_debug=race_debug, _fences=fences)


_CACHE = {}


def _host_layout(L0_attn_x, L0_attn_w_qkv, L0_attn_w_out, L0_ffn_norm_scale,
                 L0_ffn_gate_w, L0_ffn_gate_b, L0_ffn_mlp1_w, L0_ffn_mlp1_b,
                 L0_ffn_mlp2_w, L0_ffn_mlp2_b):
    f = np.ascontiguousarray
    x = np.asarray(L0_attn_x, dtype=np.float32)
    shared = {
        "wqkv": f(np.asarray(L0_attn_w_qkv, np.float32)),
        "wout": f(np.asarray(L0_attn_w_out, np.float32)),
        "ns": f(np.asarray(L0_ffn_norm_scale, np.float32).reshape(KC, 128).T),
        "gatew": f(np.asarray(L0_ffn_gate_w, np.float32).reshape(KC, 128, E)
                   .transpose(1, 0, 2).reshape(128, KC * E)),
        "gateb": f(np.tile(np.asarray(L0_ffn_gate_b, np.float32)[None, :], (128, 1))),
        "wglu": f(np.asarray(L0_ffn_mlp1_w, np.float32)[:, :, 0::2]
                  .reshape(E, KC, 128, NP, 256).transpose(0, 3, 2, 1, 4)
                  .reshape(E, NP, 128, KC * 256)),
        "wlin": f(np.asarray(L0_ffn_mlp1_w, np.float32)[:, :, 1::2]
                  .reshape(E, KC, 128, NP, 256).transpose(0, 3, 2, 1, 4)
                  .reshape(E, NP, 128, KC * 256)),
        "b1g": f(np.asarray(L0_ffn_mlp1_b, np.float32)[:, 0::2].reshape(E, HC, 128)
                 .transpose(0, 2, 1)),
        "b1l": f(np.asarray(L0_ffn_mlp1_b, np.float32)[:, 1::2].reshape(E, HC, 128)
                 .transpose(0, 2, 1)),
        "w2": f(np.asarray(L0_ffn_mlp2_w, np.float32)),
        "b2": f(np.asarray(L0_ffn_mlp2_b, np.float32)),
    }
    return x, shared


def kernel(**inputs):
    x, shared = _host_layout(**inputs)
    if "nc" not in _CACHE:
        _CACHE["nc"] = build_nc_fenced()
    nc = _CACHE["nc"]
    in_maps = [dict(shared, x=np.ascontiguousarray(x[b])) for b in range(8)]
    res = run_bass_kernel_spmd(nc, in_maps, core_ids=list(range(8)))
    return np.stack([np.asarray(res.results[b]["y"], np.float32) for b in range(8)], axis=0)
